# version 11
# baseline (speedup 1.0000x reference)
import contextlib
import numpy as np
import concourse.bass as bass
import concourse.mybir as mybir
from concourse.bass_utils import run_bass_kernel_spmd
from concourse.alu_op_type import AluOpType as OP

F32 = mybir.dt.float32
BF16 = mybir.dt.bfloat16
AF = mybir.ActivationFunctionType

ENGINES = ("tensor", "vector", "scalar", "gpsimd", "sync")

T, D, N, DFF, E, I = 4096, 1024, 16, 4096, 32, 512
CH, G = 512, 1024
NG, NKD, NTAU = T // G, D // 128, CH // 128


class SemT:
    __slots__ = ("name", "h", "c")

    def __init__(self, name):
        self.name = name
        self.h = None
        self.c = 0


class Prog:
    def __init__(self, nc):
        self.nc = nc
        self.ops = {e: [] for e in ENGINES}
        self.sems = []
        self._waited = {e: {} for e in ENGINES}

    def sem(self, name):
        s = SemT(name)
        self.sems.append(s)
        return s

    def on(self, eng, fn, inc=None, n=1, drain=False):
        if drain:
            self.ops[eng].append(fn)
            if inc is not None:
                inc.c += n
                self.ops[eng].append(
                    lambda e, inc=inc, n=n: e.drain().then_inc(inc.h, n))
            else:
                self.ops[eng].append(lambda e: e.drain())
        elif inc is not None:
            inc.c += n

            def wrapped(e, fn=fn, inc=inc, n=n):
                fn(e).then_inc(inc.h, n)
            self.ops[eng].append(wrapped)
        else:
            self.ops[eng].append(fn)

    def wait(self, eng, sem, at=None):
        tgt = sem.c if at is None else at
        if tgt <= 0:
            return
        prev = self._waited[eng].get(sem, 0)
        if tgt <= prev:
            return
        self._waited[eng][sem] = tgt
        self.ops[eng].append(lambda e, s=sem, t=tgt: e.wait_ge(s.h, t))

    def wait_mark(self, eng, mark):
        if mark is not None:
            self.wait(eng, mark[0], mark[1])

    def build(self, stack):
        nc = self.nc
        for s in self.sems:
            s.h = stack.enter_context(nc.semaphore(s.name))
        block = stack.enter_context(nc.Block())
        for eng_name in ENGINES:
            lst = self.ops[eng_name]
            if not lst:
                continue

            def section(engine, lst=lst):
                for f in lst:
                    f(engine)
            getattr(block, eng_name)(section)


def build_nc():
    nc = bass.Bass("TRN2", target_bir_lowering=False, debug=False)

    xin = nc.dram_tensor("x", [T, D], F32, kind="ExternalInput").ap()
    out = nc.dram_tensor("out", [T, D], F32, kind="ExternalOutput").ap()

    def din(name, shape):
        return nc.dram_tensor(name, shape, F32, kind="ExternalInput").ap()

    n1w = din("norm1_w", [D]); n2w = din("norm2_w", [D])
    A_log = din("A_log", [N]); D_param = din("D_param", [D]); p_scale = din("scale", [D])
    Wdt = din("Wdt", [D, D]); bdt = din("bdt", [D])
    WB = din("WB", [N, D]); bB = din("bB", [N])
    WC = din("WC", [N, D]); bC = din("bC", [N])
    W1 = din("W1", [DFF, D]); b1 = din("b1", [DFF])
    W2 = din("W2", [D, DFF]); b2 = din("b2", [D])
    Wr = din("Wr", [D, E]); br = din("br", [E])
    Wg_s = din("Wg_s", [D, I]); bg_s = din("bg_s", [I])
    Wu_s = din("Wu_s", [D, I]); bu_s = din("bu_s", [I])
    Wd_s = din("Wd_s", [I, D]); bd_s = din("bd_s", [D])
    Wg = din("Wg", [E, D, I]); bg = din("bg", [E, I])
    Wu = din("Wu", [E, D, I]); bu = din("bu", [E, I])
    Wd = din("Wd", [E, I, D]); bd = din("bd", [E, D])

    Wdt_b = nc.dram_tensor("Wdt_b", [D, D], BF16, kind="Internal").ap()
    W1_b = nc.dram_tensor("W1_b", [DFF, D], BF16, kind="Internal").ap()
    W2_b = nc.dram_tensor("W2_b", [D, DFF], BF16, kind="Internal").ap()
    WdtT_b = nc.dram_tensor("WdtT_b", [D, D], BF16, kind="Internal").ap()
    W1T_b = nc.dram_tensor("W1T_b", [D, DFF], BF16, kind="Internal").ap()
    W2T_b = nc.dram_tensor("W2T_b", [DFF, D], BF16, kind="Internal").ap()

    pr = Prog(nc)
    dsm = pr.sem("dsm")
    dmg = pr.sem("dmg")
    dx = pr.sem("dx")
    dwdt = [pr.sem(f"dwdt{i}") for i in range(2)]
    dq = [pr.sem(f"dq{i}") for i in range(2)]
    dgu = [pr.sem(f"dgu{i}") for i in range(4)]
    dwd = [pr.sem(f"dwd{i}") for i in range(2)]
    dpl = [pr.sem(f"dpl{i}") for i in range(2)]
    dps = [pr.sem(f"dps{i}") for i in range(2)]
    dms = pr.sem("dms")
    st = pr.sem("st")
    sv = pr.sem("sv")
    ss = pr.sem("ss")
    sg = pr.sem("sg")

    stack = contextlib.ExitStack()
    Ecm = stack.enter_context

    def sb(name, shape, dt=F32):
        return Ecm(nc.sbuf_tensor(name, shape, dt))

    acc = sb("acc", [128, (G // 128) * D])
    xnT_bf = sb("xnT_bf", [128, NKD * G], BF16)
    xn3T_bf = sb("xn3T_bf", [128, NKD * G], BF16)
    w_sb = sb("w_sb", [128, (G // 128) * E])
    wT_sb = sb("wT_sb", [E, G], BF16)
    Wr_sb = sb("Wr_sb", [128, NKD * E])
    WBT = sb("WBT", [128, NKD * N], BF16); WCT = sb("WCT", [128, NKD * N], BF16)
    bd_sb = sb("bd_sb", [E, D], BF16)
    b2_r = sb("b2_r", [1, D], BF16); bds_r = sb("bds_r", [1, D], BF16)
    ones_b = sb("ones_b", [1, 512], BF16)
    bB_r = sb("bB_r", [1, N]); bC_r = sb("bC_r", [1, N]); br_r = sb("br_r", [1, E])
    n1w_c = sb("n1w_c", [128, NKD]); n2w_c = sb("n2w_c", [128, NKD])
    Dp_c = sb("Dp_c", [128, NKD]); sc_c = sb("sc_c", [128, NKD])
    bdt_c = sb("bdt_c", [128, NKD])
    b1_c = sb("b1_c", [128, DFF // 128])
    bg_c = sb("bg_c", [128, E * 4]); bu_c = sb("bu_c", [128, E * 4])
    bgs_c = sb("bgs_c", [128, 4]); bus_c = sb("bus_c", [128, 4])
    A_bc = sb("A_bc", [128, N])
    recipA = sb("recipA", [N, 1])
    ones_r = sb("ones_r", [1, 512])
    ones_c = sb("ones_c", [128, 1])
    ident = sb("ident", [128, 128], F32)
    A_lr = sb("A_lr", [1, N]); A_lc = sb("A_lc", [N, 1])
    tmpA = sb("tmpA", [1, N]); tmpA2 = sb("tmpA2", [N, 1])
    stat = sb("stat", [128, 8])
    aff = sb("aff", [128, E]); top8 = sb("top8", [128, 8]); work = sb("work", [128, E])
    maskb = sb("maskb", [128, E]); wsel = sb("wsel", [128, E])
    ssumc = sb("ssumc", [128, 1]); sinvc = sb("sinvc", [128, 1])
    Bv_sb = sb("Bv_sb", [N, CH], BF16); Cv_sb = sb("Cv_sb", [N, CH], BF16)
    r2s = sb("r2s", [1, CH])
    state = sb("state", [128, 128])
    xnf = sb("xnf", [128, CH])
    rp_sb = sb("rp_sb", [128, 2 * E])
    bsil = sb("bsil", [128, 512], BF16); btmp = sb("btmp", [128, 512], BF16)
    wb_sb = sb("wb_sb", [128, 1024], BF16)
    constv = sb("constv", [128, 4])
    sq_big = sb("sq_big", [128, D])
    sel32 = sb("sel32", [E, E * 128], BF16)

    AR = 51200
    arena = sb("arena", [128, AR], BF16)
    x2T_b = arena[:, 0:4096]
    dt_b = arena[:, 4096:8192]
    xr_b = arena[:, 8192:12288]
    ya = arena[:, 12288:20480].bitcast(F32)
    xn_s = arena[:, 20480:22528].bitcast(F32)
    dA0 = arena[:, 22528:23040]
    dA1 = arena[:, 23040:23552]
    q0_sb = arena[:, 23552:24064]
    q1_sb = arena[:, 24064:24576]
    h0_sb = arena[:, 24576:25088]
    h1_sb = arena[:, 25088:25600]
    bt0_sb = arena[:, 25600:26112]
    bt1_sb = arena[:, 26112:26624]
    sp_a = arena[:, 26624:27648].bitcast(F32)
    sp_b = arena[:, 27648:28672].bitcast(F32)
    sp_c = arena[:, 28672:29696].bitcast(F32)
    sqt = sp_c
    xnr_h = sp_a
    WdtT0 = arena[:, 29696:30720]
    WdtT1 = WdtT0
    q_sb = sp_c
    WGU0 = 30720
    WD0 = 38912
    V0 = 43008

    def wgu_slot(k):
        b = WGU0 + (k % 4) * 2048
        return arena[:, b:b + 1024], arena[:, b + 1024:b + 2048]

    def wd_slot(k):
        b = WD0 + (k % 2) * 2048
        return arena[:, b:b + 2048]

    def v_slot(k):
        b = V0 + (k % 2) * 4096
        return arena[:, b:b + 4096]

    def mlpq(p):
        base = 30720 + p * 8192
        return arena[:, base:base + 4096], arena[:, base + 4096:base + 8192]

    hTm = arena[:, 47104:51200]

    tmp_w_a = xn3T_bf[:, 0:4096]
    tmp_w_b = xn3T_bf[:, 4096:8192]

    pb = [Ecm(nc.psum_tensor(f"pb{i}", [128, 512], F32)) for i in range(8)]

    ready = {}
    freed = {}

    def mark(d, key, sem):
        d[key] = (sem, sem.c)

    def need(eng, d, key):
        if key in d:
            pr.wait(eng, d[key][0], d[key][1])

    def bank_free(i, sem):
        mark(freed, f"pb{i}", sem)

    def need_bank(i):
        need("tensor", freed, f"pb{i}")

    def mm(o, l, r, start, stop, inc=False):
        pr.on("tensor", lambda e, o=o, l=l, r=r, s=start, t=stop:
              e.matmul(o, l, r, start=s, stop=t), inc=(st if inc else None))

    def tr(o, a, inc=False):
        pr.on("tensor", lambda e, o=o, a=a: e.transpose(o, a, ident[:, :]),
              inc=(st if inc else None))

    def v_tt(o, a, b, op, inc=False, nd=False):
        pr.on("vector", lambda e, o=o, a=a, b=b, op=op: e.tensor_tensor(
            out=o, in0=a, in1=b, op=op), inc=(sv if inc else None), drain=not nd)

    def v_ts(o, a, s1, s2, op0, op1=None, inc=False):
        if op1 is None:
            pr.on("vector", lambda e, o=o, a=a, s1=s1, op0=op0: e.tensor_scalar(
                out=o, in0=a, scalar1=s1, scalar2=None, op0=op0),
                inc=(sv if inc else None), drain=True)
        else:
            pr.on("vector", lambda e, o=o, a=a, s1=s1, s2=s2, op0=op0, op1=op1:
                  e.tensor_scalar(out=o, in0=a, scalar1=s1, scalar2=s2, op0=op0,
                                  op1=op1), inc=(sv if inc else None), drain=True)

    def v_stt(o, a, s, b, op0, op1, inc=False, nd=False):
        pr.on("vector", lambda e, o=o, a=a, s=s, b=b, op0=op0, op1=op1:
              e.scalar_tensor_tensor(out=o, in0=a, scalar=s, in1=b, op0=op0,
                                     op1=op1), inc=(sv if inc else None), drain=not nd)

    def v_copy(o, a, inc=False):
        pr.on("vector", lambda e, o=o, a=a: e.tensor_copy(out=o, in_=a),
              inc=(sv if inc else None), drain=True)

    def v_recip(o, a, inc=False):
        pr.on("vector", lambda e, o=o, a=a: e.reciprocal(out=o, in_=a),
              inc=(sv if inc else None), drain=True)

    def v_red(o, a, op, inc=False):
        pr.on("vector", lambda e, o=o, a=a, op=op: e.tensor_reduce(
            out=o, in_=a, axis=mybir.AxisListType.X, op=op),
            inc=(sv if inc else None), drain=True)

    def s_act(o, a, f, bias=None, scale=None, accum=None, inc=True, nd=False):
        kw = {}
        if bias is not None:
            kw["bias"] = bias
        if scale is not None:
            kw["scale"] = scale
        if accum is not None:
            kw["accum_out"] = accum
        pr.on("scalar", lambda e, o=o, a=a, f=f, kw=kw: e.activation(
            out=o, in_=a, func=f, **kw), inc=(ss if inc else None), drain=not nd)

    def v_drain(inc=False):
        if inc:
            sv.c += 1
            pr.ops["vector"].append(lambda e: e.drain().then_inc(sv.h, 1))
        else:
            pr.ops["vector"].append(lambda e: e.drain())

    def s_drain(inc=False):
        if inc:
            ss.c += 1
            pr.ops["scalar"].append(lambda e: e.drain().then_inc(ss.h, 1))
        else:
            pr.ops["scalar"].append(lambda e: e.drain())

    def s_copy(o, a, inc=True, nd=False):
        pr.on("scalar", lambda e, o=o, a=a: e.copy(out=o, in_=a),
              inc=(ss if inc else None), drain=not nd)

    def fence(*engs):
        sems = {"tensor": st, "vector": sv, "scalar": ss, "gpsimd": sg}
        marks = {}
        for eng in engs:
            s = sems[eng]
            s.c += 1
            pr.ops[eng].append(lambda e, s=s: e.drain().then_inc(s.h, 1))
            marks[eng] = (s, s.c)
        return marks

    def wait_fence(eng, marks):
        if marks:
            for m_ in marks.values():
                pr.wait_mark(eng, m_)

    def L(dst, src, slow=False):
        pr.on("sync", lambda e, d=dst, s=src, sl=slow: e.dma_start(
            out=d, in_=s, allow_slow_non_contiguous=sl), inc=dsm, n=16)

    def Lg(dst, src, slow=False):
        pr.on("gpsimd", lambda e, d=dst, s=src, sl=slow: e.dma_start(
            out=d, in_=s, allow_slow_non_contiguous=sl), inc=dmg, n=16)

    L(n1w_c[:, :], n1w.rearrange("(c p) -> p c", p=128), True)
    L(n2w_c[:, :], n2w.rearrange("(c p) -> p c", p=128), True)
    L(Dp_c[:, :], D_param.rearrange("(c p) -> p c", p=128), True)
    L(sc_c[:, :], p_scale.rearrange("(c p) -> p c", p=128), True)
    L(bdt_c[:, :], bdt.rearrange("(c p) -> p c", p=128), True)
    L(b1_c[:, :], b1.rearrange("(c p) -> p c", p=128), True)
    for e_ in range(E):
        L(bg_c[:, e_ * 4:(e_ + 1) * 4], bg[e_].rearrange("(c p) -> p c", p=128), True)
        L(bu_c[:, e_ * 4:(e_ + 1) * 4], bu[e_].rearrange("(c p) -> p c", p=128), True)
    L(bgs_c[:, :], bg_s.rearrange("(c p) -> p c", p=128), True)
    L(bus_c[:, :], bu_s.rearrange("(c p) -> p c", p=128), True)
    L(A_lr[:, :], A_log[None, :])
    L(A_lc[:, :], A_log[:, None])
    for c_ in range(NKD):
        L(Wr_sb[:, c_ * E:(c_ + 1) * E], Wr[c_ * 128:(c_ + 1) * 128, :])
    L(bB_r[:, :], bB[None, :]); L(bC_r[:, :], bC[None, :]); L(br_r[:, :], br[None, :])
    small_loads = (dsm, dsm.c)
    Lg(bd_sb[:, :], bd)
    Lg(b2_r[:, :], b2[None, :]); Lg(bds_r[:, :], bd_s[None, :])
    for c_ in range(NKD):
        Lg(WBT[:, c_ * N:(c_ + 1) * N],
           WB[:, c_ * 128:(c_ + 1) * 128].rearrange("n p -> p n"), True)
        Lg(WCT[:, c_ * N:(c_ + 1) * N],
           WC[:, c_ * 128:(c_ + 1) * 128].rearrange("n p -> p n"), True)
    pr.on("gpsimd", lambda e: e.dma_start(out=Wdt_b, in_=Wdt), inc=dmg, n=16)
    pr.on("gpsimd", lambda e: e.dma_start(out=W1_b, in_=W1), inc=dmg, n=16)
    pr.on("gpsimd", lambda e: e.dma_start(out=W2_b, in_=W2), inc=dmg, n=16)
    casts_done = (dmg, dmg.c)
    wbct_loads = (dmg, dmg.c)

    for f_ in [lambda e: e.memset(constv[:, 0:1], 1e-5),
               lambda e: e.memset(constv[:, 1:2], 1e-6),
               lambda e: e.memset(constv[:, 2:3], 1.0 / D),
               lambda e: e.memset(ones_r[:, :], 1.0),
               lambda e: e.memset(ones_b[:, :], 1.0),
               lambda e: e.memset(ones_c[:, :], 1.0),
               lambda e: e.memset(state[:, :], 0.0),
               lambda e: e.memset(ident[:, :], 0.0),
               lambda e: e.memset(sel32[:, :], 1.0),
               lambda e: e.affine_select(
                   out=sel32[:, :], in_=sel32[:, :], compare_op=OP.is_ge, fill=0.0,
                   base=0, pattern=[[1, E * 128]], channel_multiplier=-128),
               lambda e: e.affine_select(
                   out=sel32[:, :], in_=sel32[:, :], compare_op=OP.is_ge, fill=0.0,
                   base=127, pattern=[[-1, E * 128]], channel_multiplier=128)]:
        pr.on("gpsimd", f_, drain=True)
    pr.on("gpsimd", lambda e: e.affine_select(
        out=ident[:, :], in_=ident[:, :], compare_op=OP.not_equal,
        fill=1.0, base=0, pattern=[[-1, 128]], channel_multiplier=1),
        inc=sg, drain=True)
    consts_ready = (sg, sg.c)

    pr.wait_mark("scalar", small_loads)
    s_act(tmpA[:, :], A_lr[:, :], AF.Exp)
    s_act(tmpA2[:, :], A_lc[:, :], AF.Exp, scale=-1.0)
    pr.wait("vector", ss, ss.c)
    v_ts(tmpA[:, :], tmpA[:, :], -1.0, None, OP.mult, inc=True)
    v_ts(recipA[:, :], tmpA2[:, :], -1.0, None, OP.mult, inc=True)
    A_row_ready = (sv, sv.c)
    pr.wait_mark("tensor", consts_ready)
    pr.wait_mark("tensor", A_row_ready)
    mm(pb[0][:, 0:N], ones_r[0:1, 0:128], tmpA[0:1, :], True, True, inc=True)
    pr.wait("vector", st, st.c)
    v_copy(A_bc[:, :], pb[0][:, 0:N], inc=True)
    bank_free(0, sv)

    pr.wait_mark("sync", casts_done)

    glb = {"blk": 0}

    def prep_transpose(src, dst, nblk, blkrows):
        for b_ in range(nblk):
            par = glb["blk"] % 2
            glb["blk"] += 1
            buf = (tmp_w_a if par == 0 else tmp_w_b)[:, 0:blkrows]
            pr.wait("sync", dps[par], dps[par].c)
            pr.on("sync", lambda e, o=buf, s=src, b_=b_: e.dma_start_transpose(
                out=o, in_=s[:, b_ * 128:(b_ + 1) * 128]), inc=dpl[par], n=16)
            pr.wait("sync", dpl[par], dpl[par].c)
            pr.on("sync", lambda e, o=dst, i=buf, b_=b_: e.dma_start(
                out=o[b_ * 128:(b_ + 1) * 128, :], in_=i), inc=dps[par], n=16)

    prep_transpose(Wdt_b, WdtT_b, NKD, D)
    prep_transpose(W1_b, W1T_b, NKD, DFF)
    prep_transpose(W2_b, W2T_b, DFF // 128, D)
    prep_stores = [(dps[0], dps[0].c), (dps[1], dps[1].c)]
    freed["xn3T_bf"] = prep_stores[0]
    freed["xn3T_bf2"] = prep_stores[1]

    nE = E + 1
    pairs = [list(range(p_, min(p_ + 2, nE))) for p_ in range(0, nE, 2)]

    def make_B_units(g):
        units = []
        gu_idx = {"n": 0}

        def load_gu(k):
            e_, it = divmod(k, 4)
            if e_ >= nE:
                return
            wgt, wut = wgu_slot(k)
            sgu = dgu[k % 4]
            sg_, su_ = ((Wg[e_], Wu[e_]) if e_ < E else (Wg_s, Wu_s))

            def em():
                need("gpsimd", freed, f"gu{k % 4}")
                pr.on("gpsimd", lambda e, o=wgt, s=sg_, it=it: e.dma_start(
                    out=o.rearrange("p (hc i2) -> p hc i2", i2=128),
                    in_=s.rearrange("(hc p) i -> p hc i", p=128)[
                        :, :, it * 128:(it + 1) * 128],
                    allow_slow_non_contiguous=True), inc=sgu, n=16)
                pr.on("gpsimd", lambda e, o=wut, s=su_, it=it: e.dma_start(
                    out=o.rearrange("p (hc i2) -> p hc i2", i2=128),
                    in_=s.rearrange("(hc p) i -> p hc i", p=128)[
                        :, :, it * 128:(it + 1) * 128],
                    allow_slow_non_contiguous=True), inc=sgu, n=16)
                ready[f"gur{k % 4}"] = (sgu, sgu.c)
            return em

        def load_wd_e(e_, dh):
            wdt_ = wd_slot(e_)
            sdd = dwd[e_ % 2]
            src_ = Wd[e_] if e_ < E else Wd_s

            def em():
                need("gpsimd", freed, f"wd{e_ % 2}")
                pr.on("gpsimd", lambda e, o=wdt_, s=src_, dh=dh: e.dma_start(
                    out=o.rearrange("p (ic h2) -> p ic h2", h2=512),
                    in_=s.rearrange("(ic p) h -> p ic h", p=128)[
                        :, :, dh * 512:(dh + 1) * 512],
                    allow_slow_non_contiguous=True), inc=sdd, n=16)
                ready[f"wdr{e_ % 2}"] = (sdd, sdd.c)
            return em

        def gu_unit(e_, it):
            k = e_ * 4 + it
            wgt, wut = wgu_slot(k)

            def em():
                pr.wait_mark("tensor", ready.get(f"gur{k % 4}"))
                pr.wait_mark("tensor", ready["xnT_g"])
                if e_ < E and it == 0:
                    pr.wait_mark("tensor", ready["w_g"])
                    need_bank(6); need_bank(7)
                    mm(pb[6][:, :], sel32[0:E, e_ * 128:(e_ + 1) * 128],
                       wT_sb[0:E, 0:512], True, True)
                    mm(pb[7][:, :], sel32[0:E, e_ * 128:(e_ + 1) * 128],
                       wT_sb[0:E, 512:1024], True, True, inc=True)
                    wbm = st.c
                    pr.wait("scalar", st, wbm)
                    need("scalar", freed, "wb_sb")
                    s_copy(wb_sb[:, 0:512], pb[6][:, :], inc=False, nd=True)
                    s_copy(wb_sb[:, 512:1024], pb[7][:, :])
                    bank_free(6, ss); bank_free(7, ss)
                    ready["wb"] = (ss, ss.c)
                need_bank(2); need_bank(3); need_bank(4); need_bank(5)
                for th in range(2):
                    for hc in range(NKD):
                        mm(pb[2 + th][:, :], wgt[:, hc * 128:(hc + 1) * 128],
                           xnT_bf[:, hc * G + th * 512:hc * G + (th + 1) * 512],
                           hc == 0, hc == NKD - 1)
                for th in range(2):
                    for hc in range(NKD):
                        mm(pb[4 + th][:, :], wut[:, hc * 128:(hc + 1) * 128],
                           xnT_bf[:, hc * G + th * 512:hc * G + (th + 1) * 512],
                           hc == 0, hc == NKD - 1,
                           inc=(th == 1 and hc == NKD - 1))
                gum = st.c
                pr.wait("scalar", st, gum)
                need("scalar", freed, "bsil")
                gcol = (bg_c[:, e_ * 4 + it:e_ * 4 + it + 1] if e_ < E
                        else bgs_c[:, it:it + 1])
                ucol = (bu_c[:, e_ * 4 + it:e_ * 4 + it + 1] if e_ < E
                        else bus_c[:, it:it + 1])
                s_act(bsil[:, :], pb[2][:, :], AF.Silu, bias=gcol, inc=False, nd=True)
                s_act(btmp[:, :], pb[3][:, :], AF.Silu, bias=gcol, inc=True)
                bank_free(2, ss); bank_free(3, ss)
                pr.wait("vector", ss, ss.c)
                if e_ < E:
                    pr.wait_mark("vector", ready["wb"])
                need("vector", freed, f"v{e_ % 2}")
                vt0 = v_slot(e_)[:, it * 1024: it * 1024 + 512]
                vt1 = v_slot(e_)[:, it * 1024 + 512: it * 1024 + 1024]
                if e_ < E:
                    v_tt(bsil[:, :], bsil[:, :], wb_sb[:, 0:512], OP.mult, nd=True)
                    v_tt(btmp[:, :], btmp[:, :], wb_sb[:, 512:1024], OP.mult)
                    v_stt(vt0, pb[4][:, :], ucol, bsil[:, :], OP.add, OP.mult, nd=True)
                    v_stt(vt1, pb[5][:, :], ucol, btmp[:, :], OP.add, OP.mult,
                          inc=True)
                else:
                    v_stt(vt0, pb[4][:, :], ucol, bsil[:, :], OP.add, OP.mult, nd=True)
                    v_stt(vt1, pb[5][:, :], ucol, btmp[:, :], OP.add, OP.mult,
                          inc=True)
                mark(freed, "bsil", sv)
                bank_free(4, sv); bank_free(5, sv)
                if e_ < E and it == 3:
                    mark(freed, "wb_sb", sv)
                if it == 3:
                    ready[f"vr{e_ % 2}"] = (sv, sv.c)
                mark(freed, f"gu{k % 4}", sv)
                nxt = load_gu(k + 4)
                if nxt is not None:
                    nxt()
            return em

        def down_unit(pi, pair, dh, t_, dpar):
            def em():
                for e_ in pair:
                    pr.wait_mark("tensor", ready.get(f"vr{e_ % 2}"))
                    pr.wait_mark("tensor", ready.get(f"wdr{e_ % 2}"))
                th, tq = t_ // 4, t_ % 4
                bnk = 6 + dpar
                need_bank(bnk)
                first = True
                if pi == 0:
                    pr.wait_mark("tensor", ready["w_g"])
                    mm(pb[bnk][:, :],
                       wT_sb[0:E, th * 512 + tq * 128: th * 512 + (tq + 1) * 128],
                       bd_sb[0:E, dh * 512:(dh + 1) * 512], True, False)
                    first = False
                if nE - 1 in pair:
                    mm(pb[bnk][:, :], ones_b[0:1, 0:128],
                       bds_r[0:1, dh * 512:(dh + 1) * 512], first, False)
                    first = False
                for e_ in pair:
                    wdt_ = wd_slot(e_)
                    for it in range(4):
                        vt = v_slot(e_)[:, it * 1024 + th * 512 + tq * 128:
                                        it * 1024 + th * 512 + (tq + 1) * 128]
                        last = (e_ == pair[-1]) and (it == 3)
                        mm(pb[bnk][:, :], vt,
                           wdt_[:, it * 512:(it + 1) * 512],
                           first, last, inc=last)
                        first = False
                pr.wait("vector", st, st.c)
                av = acc[:, t_ * D + dh * 512: t_ * D + (dh + 1) * 512]
                need("vector", freed, "acc_v")
                v_tt(av, av, pb[bnk][:, :], OP.add, inc=True)
                bank_free(bnk, sv)
                if t_ == 7:
                    for e_ in pair:
                        mark(freed, f"wd{e_ % 2}", st)
                        if dh == 1:
                            mark(freed, f"v{e_ % 2}", st)
            return em

        dpar = {"p": 0}
        for pi, pair in enumerate(pairs):
            for e_ in pair:
                units.append((load_wd_e(e_, 0), 0.3))
            for e_ in pair:
                for it in range(4):
                    units.append((gu_unit(e_, it), 7.0))
            for dh in range(2):
                if dh == 1:
                    for e_ in pair:
                        units.append((load_wd_e(e_, 1), 0.3))
                for t_ in range(8):
                    units.append((down_unit(pi, pair, dh, t_, dpar["p"]), 2.4))
                    dpar["p"] = 1 - dpar["p"]
        return units, load_gu

    markMLP = None
    for g in range(NG):
        need("sync", freed, "acc")
        pr.on("sync", lambda e, g=g: e.dma_start(
            out=acc[:, :].rearrange("p (t d) -> p t d", d=D),
            in_=xin[g * G:(g + 1) * G, :].rearrange("(t p) d -> p t d", p=128)),
            inc=dx, n=16)
        x_ready = (dx, dx.c)
        pr.wait_mark("vector", small_loads)
        pr.wait_mark("scalar", small_loads)
        pr.wait_mark("vector", consts_ready)
        pr.wait_mark("scalar", consts_ready)

        B_units, load_gu = make_B_units(g)
        bu_iter = {"i": 0}

        def bslot(budget_us=18.0):
            used = 0.0
            while used < budget_us:
                i = bu_iter["i"]
                if i >= len(B_units):
                    return
                em, cost = B_units[i]
                bu_iter["i"] = i + 1
                em()
                used += cost

        def bflush():
            while bu_iter["i"] < len(B_units):
                B_units[bu_iter["i"]][0]()
                bu_iter["i"] += 1

        wait_fence("gpsimd", markMLP)
        for k in range(4):
            f_ = load_gu(k)
            if f_ is not None:
                f_()

        def emit_A12(ch2):
            tau0 = ch2 * NTAU
            need("vector", freed, "xn_s"); need("scalar", freed, "xn_s")
            need("scalar", freed, "x2T_b")
            need("scalar", freed, "xnT_bf")
            for t_ in range(NTAU):
                xv = acc[:, (tau0 + t_) * D:(tau0 + t_ + 1) * D]
                c0 = stat[:, 0:1]; c1 = stat[:, 1:2]; cx = stat[:, 2:3]
                c3 = stat[:, 3:4]; c4 = stat[:, 4:5]; c5 = stat[:, 5:6]
                r1c = stat[:, 6:7]
                pr.wait_mark("vector", x_ready)
                pr.wait_mark("scalar", x_ready)
                v_red(c0, xv, OP.add)
                s_act(sq_big[:, :], xv, AF.Square, accum=c1)
                pr.wait("vector", ss, ss.c)
                v_ts(c0, c0, 1.0 / D, None, OP.mult)
                v_ts(cx, c1, 1.0 / D, None, OP.mult)
                v_tt(c3, c0, c0, OP.mult)
                v_tt(c3, cx, c3, OP.subtract, inc=True)
                pr.wait("scalar", sv, sv.c)
                s_act(c4, c3, AF.Sqrt, bias=constv[:, 0:1])
                pr.wait("vector", ss, ss.c)
                v_recip(c5, c4)
                need("vector", freed, "xn_s")
                v_ts(xn_s[:, :], xv, c0, c5, OP.subtract, OP.mult, inc=True)
                pr.wait("scalar", sv, sv.c)
                s_act(sq_big[:, :], xn_s[:, :], AF.Square, accum=c1)
                s_act(c4, c1, AF.Sqrt, scale=constv[:, 2:3], bias=constv[:, 1:2])
                pr.wait("vector", ss, ss.c)
                v_recip(r1c, c4, inc=True)
                xn_t_ready = (sv, sv.c)

                pr.wait_mark("tensor", xn_t_ready)
                pr.wait_mark("tensor", consts_ready)
                for half in range(2):
                    need("vector", freed, "xnr_h")
                    v_ts(xnr_h[:, :], xn_s[:, half * 512:(half + 1) * 512],
                         r1c, None, OP.mult, inc=True)
                    need_bank(0); need_bank(1)
                    pr.wait("tensor", sv, sv.c)
                    for k4 in range(4):
                        kd = half * 4 + k4
                        tr(pb[0][:, k4 * 128:(k4 + 1) * 128],
                           xn_s[:, kd * 128:(kd + 1) * 128])
                    for k4 in range(4):
                        tr(pb[1][:, k4 * 128:(k4 + 1) * 128],
                           xnr_h[:, k4 * 128:(k4 + 1) * 128],
                           inc=(k4 == 3))
                    tmark = st.c
                    mark(freed, "xnr_h", st)
                    if half == 1:
                        mark(freed, "xn_s", st)
                    pr.wait("scalar", st, tmark)
                    xnT_v = xnT_bf[:, :].rearrange("p (kd t2) -> p kd t2", t2=G)[
                        :, half * 4:(half + 1) * 4,
                        ch2 * CH + t_ * 128: ch2 * CH + (t_ + 1) * 128]
                    s_copy(xnT_v, pb[0][:, :], inc=False, nd=True)
                    for k4 in range(4):
                        kd = half * 4 + k4
                        s_act(x2T_b[:, kd * CH + t_ * 128: kd * CH + (t_ + 1) * 128],
                              pb[1][:, k4 * 128:(k4 + 1) * 128], AF.Copy,
                              scale=n1w_c[:, kd:kd + 1], nd=(k4 < 3),
                              inc=(k4 == 3))
                    bank_free(1, ss)
                    pr.wait("vector", ss, ss.c)
                    need("vector", freed, "xnf")
                    v_copy(xnf[:, :], pb[0][:, :], inc=True)
                    bank_free(0, sv)
                    need_bank(0)
                    pr.wait("tensor", sv, sv.c)
                    if half == 0:
                        mm(pb[0][0:128, 0:E], ones_r[0:1, 0:128], br_r[0:1, :],
                           True, False)
                    for k4 in range(4):
                        kd = half * 4 + k4
                        mm(pb[0][0:128, 0:E], xnf[:, k4 * 128:(k4 + 1) * 128],
                           Wr_sb[:, kd * E:(kd + 1) * E], half == 1 and k4 == 0,
                           k4 == 3, inc=(k4 == 3))
                    mark(freed, "xnf", st)
                    pr.wait("scalar", st, st.c)
                    need("scalar", freed, "rp")
                    s_copy(rp_sb[:, half * E:(half + 1) * E], pb[0][0:128, 0:E])
                    bank_free(0, ss)
                pr.wait("vector", ss, ss.c)
                v_tt(rp_sb[:, 0:E], rp_sb[:, 0:E], rp_sb[:, E:2 * E], OP.add, inc=True)
                mark(freed, "rp", sv)
                pr.wait("scalar", sv, sv.c)
                need("scalar", freed, "aff")
                s_act(aff[:, :], rp_sb[:, 0:E], AF.Sigmoid)
                pr.wait("vector", ss, ss.c)
                pr.on("vector", lambda e: e.max(out=top8[:, :], in_=aff[:, :]), drain=True)
                pr.on("vector", lambda e: e.match_replace(
                    out=work[:, :], in_to_replace=top8[:, :], in_values=aff[:, :],
                    imm_value=-1.0), drain=True)
                v_ts(maskb[:, :], work[:, :], -1.0, None, OP.is_equal)
                v_tt(wsel[:, :], aff[:, :], maskb[:, :], OP.mult, inc=True)
                mark(freed, "aff", sv)
                v_red(ssumc[:, 0:1], top8[:, :], OP.add)
                v_ts(ssumc[:, 0:1], ssumc[:, 0:1], 1e-9, None, OP.add)
                v_recip(sinvc[:, 0:1], ssumc[:, 0:1])
                wv = w_sb[:, (tau0 + t_) * E:(tau0 + t_ + 1) * E]
                need("vector", freed, "w_sb")
                v_ts(wv, wsel[:, :], sinvc[:, 0:1], None, OP.mult, inc=True)

            need_bank(0)
            pr.wait("tensor", sv, sv.c)
            for t_ in range(NTAU):
                tr(pb[0][0:E, t_ * 128:(t_ + 1) * 128],
                   w_sb[:, (tau0 + t_) * E:(tau0 + t_ + 1) * E],
                   inc=(t_ == NTAU - 1))
            pr.wait("vector", st, st.c)
            need("vector", freed, "wT_sb")
            v_copy(wT_sb[:, ch2 * CH:(ch2 + 1) * CH], pb[0][0:E, 0:CH], inc=True)
            bank_free(0, sv)
            if ch2 == 1:
                ready["w_g"] = (sv, sv.c)
                ready["xnT_g"] = (ss, ss.c)

        def emit_dtbv(ch2, with_b):
            x2T_ready = (ss, ss.c)
            pr.wait_mark("tensor", x2T_ready)
            need("vector", freed, "dt_b"); need("vector", freed, "xr_b")
            need("vector", freed, "sqt")
            need("scalar", freed, "xnr_h")
            for kd in range(NKD):
                wslice = WdtT0
                sw = dwdt[0]
                need("sync", freed, "wdt")
                if g == 0 and ch2 == 0 and kd < 2:
                    for m_ in prep_stores:
                        pr.wait_mark("sync", m_)
                pr.on("sync", lambda e, o=wslice, kd=kd: e.dma_start(
                    out=o.rearrange("p (c d2) -> p c d2", d2=128),
                    in_=WdtT_b.rearrange("(c p) d2 -> p c d2", p=128)[
                        :, :, kd * 128:(kd + 1) * 128],
                    allow_slow_non_contiguous=True), inc=sw, n=16)
                pr.wait_mark("tensor", (sw, sw.c))
                bnk = kd % 2
                need_bank(bnk)
                for ci in range(NKD):
                    mm(pb[bnk][:, :], wslice[:, ci * 128:(ci + 1) * 128],
                       x2T_b[:, ci * CH:(ci + 1) * CH], ci == 0, ci == NKD - 1,
                       inc=(ci == NKD - 1))
                mark(freed, "wdt", st)
                pr.wait("scalar", st, st.c)
                need("scalar", freed, "spbc")
                s_act(sp_a[:, :], pb[bnk][:, :], AF.Abs, bias=bdt_c[:, kd:kd + 1],
                      inc=False)
                s_act(sp_b[:, :], pb[bnk][:, :], AF.Relu, bias=bdt_c[:, kd:kd + 1],
                      inc=True)
                bank_free(bnk, ss)
                s_act(sp_a[:, :], sp_a[:, :], AF.Exp, scale=-1.0, inc=False)
                s_act(sp_c[:, :], sp_a[:, :], AF.Ln, bias=1.0, inc=True)
                pr.wait("vector", ss, ss.c)
                dsl = dt_b[:, kd * CH:(kd + 1) * CH]
                v_tt(dsl, sp_b[:, :], sp_c[:, :], OP.add)
                v_recip(q_sb[:, 0:CH], dsl)
                v_tt(xr_b[:, kd * CH:(kd + 1) * CH], q_sb[:, 0:CH],
                     x2T_b[:, kd * CH:(kd + 1) * CH], OP.mult, inc=True)
                mark(freed, "spbc", sv)
                if with_b and kd % 2 == 1:
                    bslot(6.0)
            dt_ready = (sv, sv.c)

            need_bank(0); need_bank(1)
            pr.wait_mark("tensor", wbct_loads)
            pr.wait_mark("tensor", dt_ready)
            for ci in range(NKD):
                mm(pb[0][0:N, 0:CH], WBT[:, ci * N:(ci + 1) * N],
                   x2T_b[:, ci * CH:(ci + 1) * CH], ci == 0, False)
            mm(pb[0][0:N, 0:CH], bB_r[0:1, :], ones_r[0:1, 0:CH], False, True, inc=True)
            for ci in range(NKD):
                mm(pb[1][0:N, 0:CH], WCT[:, ci * N:(ci + 1) * N],
                   x2T_b[:, ci * CH:(ci + 1) * CH], ci == 0, False)
            mm(pb[1][0:N, 0:CH], bC_r[0:1, :], ones_r[0:1, 0:CH], False, True, inc=True)
            bcv = st.c
            pr.wait("scalar", st, bcv)
            need("scalar", freed, "BvCv")
            s_copy(Bv_sb[:, :], pb[0][0:N, 0:CH])
            bank_free(0, ss)
            pr.wait("vector", st, bcv)
            pr.on("vector", lambda e: e.tensor_scalar(
                out=Cv_sb[:, :], in0=pb[1][0:N, 0:CH], scalar1=recipA[:, 0:1],
                scalar2=None, op0=OP.mult), inc=sv, drain=True)
            bank_free(1, sv)

            need("vector", freed, "ya")
            dcol_ap = bass.AP(Dp_c, 0, [[NKD, 128], [1, NKD], [0, CH]])
            pr.on("vector", lambda e, ap=dcol_ap: e.tensor_tensor(
                out=ya[:, :], in0=x2T_b[:, :], in1=ap, op=OP.mult), inc=sv, drain=True)
            return dt_ready, (ss, ss.c), (sv, sv.c)

        def emit_scan(ch2, dt_ready, bv_ready, cv_ready):
            pr.wait_mark("tensor", bv_ready)
            pr.wait_mark("tensor", cv_ready)
            need("vector", freed, "xnr_h")
            for n_ in range(N):
                need_bank(0); need_bank(1)
                mm(pb[0][:, :], sel32[0:N, n_ * 128:(n_ + 1) * 128],
                   Bv_sb[:, :], True, True)
                mm(pb[1][:, :], sel32[0:N, n_ * 128:(n_ + 1) * 128],
                   Cv_sb[:, :], True, True, inc=True)
                bc_ready = st.c
                pr.wait("vector", st, bc_ready)
                for kp in range(4):
                    kd0, kd1 = 2 * kp, 2 * kp + 1
                    need("scalar", freed, "dA")
                    pr.wait_mark("scalar", dt_ready)
                    s_act(dA0[:, :], dt_b[:, kd0 * CH:(kd0 + 1) * CH], AF.Exp,
                          scale=A_bc[:, n_:n_ + 1], inc=False, nd=True)
                    s_act(dA1[:, :], dt_b[:, kd1 * CH:(kd1 + 1) * CH], AF.Exp,
                          scale=A_bc[:, n_:n_ + 1], inc=True)
                    pr.wait("vector", ss, ss.c)
                    v_tt(q0_sb[:, :], pb[0][:, :], xr_b[:, kd0 * CH:(kd0 + 1) * CH],
                         OP.mult, nd=True)
                    v_tt(q1_sb[:, :], pb[0][:, :], xr_b[:, kd1 * CH:(kd1 + 1) * CH],
                         OP.mult)
                    v_stt(bt0_sb[:, :], dA0[:, :], -1.0, q0_sb[:, :], OP.add, OP.mult,
                          nd=True)
                    v_stt(bt1_sb[:, :], dA1[:, :], -1.0, q1_sb[:, :], OP.add, OP.mult)
                    col0 = state[:, n_ * 8 + kd0: n_ * 8 + kd0 + 1]
                    col1 = state[:, n_ * 8 + kd1: n_ * 8 + kd1 + 1]
                    pr.on("vector", lambda e, col=col0: e.tensor_tensor_scan(
                        out=h0_sb[:, :], data0=dA0[:, :], data1=bt0_sb[:, :],
                        initial=col, op0=OP.mult, op1=OP.add), drain=True)
                    pr.on("vector", lambda e, col=col1: e.tensor_tensor_scan(
                        out=h1_sb[:, :], data0=dA1[:, :], data1=bt1_sb[:, :],
                        initial=col, op0=OP.mult, op1=OP.add), inc=sv, drain=True)
                    mark(freed, "dA", sv)
                    pr.on("vector", lambda e, col=col0: e.tensor_copy(
                        out=col, in_=h0_sb[:, CH - 1:CH]))
                    pr.on("vector", lambda e, col=col1: e.tensor_copy(
                        out=col, in_=h1_sb[:, CH - 1:CH]))
                    v_tt(bt0_sb[:, :], h0_sb[:, :], pb[1][:, :], OP.mult, nd=True)
                    v_tt(bt1_sb[:, :], h1_sb[:, :], pb[1][:, :], OP.mult)
                    v_tt(ya[:, kd0 * CH:(kd0 + 1) * CH],
                         ya[:, kd0 * CH:(kd0 + 1) * CH], bt0_sb[:, :], OP.add, nd=True)
                    v_tt(ya[:, kd1 * CH:(kd1 + 1) * CH],
                         ya[:, kd1 * CH:(kd1 + 1) * CH], bt1_sb[:, :], OP.add,
                         inc=True)
                bank_free(0, sv); bank_free(1, sv)
                bslot(18.0)

        def emit_A567(ch2):
            tau0 = ch2 * NTAU
            for kd in range(NKD):
                ysl = ya[:, kd * CH:(kd + 1) * CH]
                v_stt(ysl, ysl, sc_c[:, kd:kd + 1],
                      xnT_bf[:, :].rearrange("p (kd t2) -> p kd t2", t2=G)[
                          :, kd, ch2 * CH:(ch2 + 1) * CH],
                      OP.mult, OP.add, nd=(kd < NKD - 1), inc=(kd == NKD - 1))
            a1_ready = (sv, sv.c)
            bslot(8.0)
            need_bank(0)
            for kd in range(NKD):
                pr.wait_mark("scalar", a1_ready)
                need("scalar", freed, "sqt")
                s_act(sqt[:, :], ya[:, kd * CH:(kd + 1) * CH], AF.Square)
                pr.wait("tensor", ss, ss.c)
                mm(pb[0][0:1, 0:CH], ones_c[:, 0:1], sqt[:, :], kd == 0, kd == NKD - 1,
                   inc=True)
                mark(freed, "sqt", st)
            pr.wait("scalar", st, st.c)
            s_act(r2s[:, :], pb[0][0:1, 0:CH], AF.Sqrt, scale=constv[0:1, 2:3],
                  bias=constv[0:1, 1:2])
            bank_free(0, ss)
            pr.wait("vector", ss, ss.c)
            v_recip(r2s[:, :], r2s[:, :], inc=True)
            need_bank(1)
            pr.wait("tensor", sv, sv.c)
            mm(pb[1][:, :], ones_r[0:1, 0:128], r2s[0:1, :], True, True, inc=True)
            pr.wait("vector", st, st.c)
            need("vector", freed, "xn3T_bf")
            need("vector", freed, "xn3T_bf2")
            for kd in range(NKD):
                v_stt(xn3T_bf[:, kd * G + ch2 * CH: kd * G + (ch2 + 1) * CH],
                      ya[:, kd * CH:(kd + 1) * CH], n2w_c[:, kd:kd + 1],
                      pb[1][:, :], OP.mult, OP.mult, nd=(kd < NKD - 1),
                      inc=(kd == NKD - 1))
            bank_free(1, sv)
            bslot(8.0)
            acc3 = acc[:, :].rearrange("p (t d) -> p t d", d=D)
            for kd in range(NKD):
                bnk = kd % 2
                need_bank(bnk)
                pr.wait_mark("tensor", a1_ready)
                for t_ in range(NTAU):
                    tr(pb[bnk][:, t_ * 128:(t_ + 1) * 128],
                       ya[:, kd * CH + t_ * 128: kd * CH + (t_ + 1) * 128],
                       inc=(t_ == NTAU - 1))
                pr.wait("vector", st, st.c)
                av = acc3[:, tau0:tau0 + NTAU, kd * 128:(kd + 1) * 128]
                need("vector", freed, "acc_v")
                v_tt(av, av, pb[bnk][:, 0:512], OP.add, inc=True)
                bank_free(bnk, sv)
                if kd % 2 == 1:
                    bslot(6.0)
            mark(freed, "ya", sv)

        emit_A12(0)
        r0 = emit_dtbv(0, with_b=False)
        emit_A12(1)
        emit_scan(0, *r0)
        emit_A567(0)
        r1 = emit_dtbv(1, with_b=True)
        emit_scan(1, *r1)
        emit_A567(1)

        bflush()
        markB = fence("tensor", "vector", "scalar")
        mark(freed, "xnT_bf", st)
        mark(freed, "w_sb", st)
        mark(freed, "wT_sb", st)
        for k in range(4):
            mark(freed, f"gu{k}", st)
        for k in range(2):
            mark(freed, f"wd{k}", st)
            mark(freed, f"v{k}", st)

        for q8 in range(8):
            w1e, w2e = mlpq(q8 % 2)
            need("sync", freed, f"mlpq{q8 % 2}")
            if q8 < 2:
                wait_fence("sync", markB)
                for m_ in prep_stores:
                    pr.wait_mark("sync", m_)
            pr.on("sync", lambda e, o=w1e, q8=q8: e.dma_start(
                out=o.rearrange("p (dc f) -> p dc f", f=512),
                in_=W1T_b[:, q8 * 512:(q8 + 1) * 512].rearrange(
                    "(dc p) f -> p dc f", p=128)), inc=dq[q8 % 2], n=16)
            pr.on("sync", lambda e, o=w2e, q8=q8: e.dma_start(
                out=o.rearrange("p (fc d) -> p fc d", d=D),
                in_=W2T_b[q8 * 512:(q8 + 1) * 512, :].rearrange(
                    "(fc p) d -> p fc d", p=128)), inc=dq[q8 % 2], n=16)
            qw_ready = (dq[q8 % 2], dq[q8 % 2].c)
            pr.wait_mark("tensor", qw_ready)
            need("scalar", freed, "hTm")
            if q8 == 0:
                wait_fence("scalar", markB)
                wait_fence("tensor", markB)
            for fcq in range(4):
                for th in range(2):
                    bnk = 2 + (fcq * 2 + th) % 2
                    need_bank(bnk)
                    for dc in range(NKD):
                        mm(pb[bnk][:, :],
                           w1e[:, dc * 512 + fcq * 128: dc * 512 + (fcq + 1) * 128],
                           xn3T_bf[:, dc * G + th * 512: dc * G + (th + 1) * 512],
                           dc == 0, dc == NKD - 1, inc=(dc == NKD - 1))
                    pr.wait("scalar", st, st.c)
                    s_act(hTm[:, fcq * G + th * 512: fcq * G + (th + 1) * 512],
                          pb[bnk][:, :], AF.Gelu_apprx_tanh,
                          bias=b1_c[:, q8 * 4 + fcq: q8 * 4 + fcq + 1])
                    bank_free(bnk, ss)
            hT_ready = (ss, ss.c)
            pr.wait_mark("tensor", hT_ready)
            for t_ in range(8):
                th, tq = t_ // 4, t_ % 4
                for dh in range(2):
                    bnk = 4 + (t_ * 2 + dh) % 2
                    need_bank(bnk)
                    first = True
                    if q8 == 0:
                        mm(pb[bnk][:, :], ones_b[0:1, 0:128],
                           b2_r[0:1, dh * 512:(dh + 1) * 512], True, False)
                        first = False
                    for fcq in range(4):
                        mm(pb[bnk][:, :],
                           hTm[:, fcq * G + th * 512 + tq * 128:
                               fcq * G + th * 512 + (tq + 1) * 128],
                           w2e[:, fcq * 1024 + dh * 512: fcq * 1024 + (dh + 1) * 512],
                           first and fcq == 0, fcq == 3, inc=(fcq == 3))
                        first = False
                    pr.wait("vector", st, st.c)
                    av = acc[:, t_ * D + dh * 512: t_ * D + (dh + 1) * 512]
                    need("vector", freed, "acc_v")
                    v_tt(av, av, pb[bnk][:, :], OP.add, inc=True)
                    bank_free(bnk, sv)
            mark(freed, "hTm", st)
            mark(freed, f"mlpq{q8 % 2}", st)
        markMLP = fence("tensor", "vector", "scalar")
        mark(freed, "xn3T_bf", st)
        mark(freed, "mlpq0", st)
        mark(freed, "mlpq1", st)

        pr.wait("sync", sv, sv.c)
        pr.on("sync", lambda e, g=g: e.dma_start(
            out=out[g * G:(g + 1) * G, :].rearrange("(t p) d -> p t d", p=128),
            in_=acc[:, :].rearrange("p (t d) -> p t d", d=D)), inc=dms, n=16)
        mark(freed, "acc", dms)
        mark(freed, "acc_v", dms)

    pr.wait("sync", dms, dms.c)
    pr.wait("gpsimd", dms, dms.c)

    pr.build(stack)
    nc._keepalive_stack = stack
    return nc


_NC_CACHE = None


def kernel(**inputs):
    global _NC_CACHE
    if _NC_CACHE is None:
        _NC_CACHE = build_nc()
    nc = _NC_CACHE
    x = np.ascontiguousarray(inputs["L0_attn_x"], dtype=np.float32)
    wmap = {}
    for k, v in inputs.items():
        if k == "L0_attn_x":
            continue
        name = k.replace("L0_attn_", "").replace("L0_ffn_", "")
        wmap[name] = np.ascontiguousarray(v, dtype=np.float32)
    in_maps = [dict(x=x[c], **wmap) for c in range(8)]
    res = run_bass_kernel_spmd(nc, in_maps, core_ids=list(range(8)))
    return np.stack([r["out"] for r in res.results], axis=0)


# version 12
# speedup vs baseline: 1.1962x; 1.1962x over previous
import contextlib
import numpy as np
import concourse.bass as bass
import concourse.mybir as mybir
from concourse.bass_utils import run_bass_kernel_spmd
from concourse.alu_op_type import AluOpType as OP

F32 = mybir.dt.float32
BF16 = mybir.dt.bfloat16
AF = mybir.ActivationFunctionType

ENGINES = ("tensor", "vector", "scalar", "gpsimd", "sync")

T, D, N, DFF, E, I = 4096, 1024, 16, 4096, 32, 512
CH, G = 512, 1024
NG, NKD, NTAU = T // G, D // 128, CH // 128


class SemT:
    __slots__ = ("name", "h", "c")

    def __init__(self, name):
        self.name = name
        self.h = None
        self.c = 0


class Prog:
    def __init__(self, nc):
        self.nc = nc
        self.ops = {e: [] for e in ENGINES}
        self.sems = []
        self._waited = {e: {} for e in ENGINES}

    def sem(self, name):
        s = SemT(name)
        self.sems.append(s)
        return s

    def on(self, eng, fn, inc=None, n=1, drain=False):
        if drain:
            self.ops[eng].append(fn)
            if inc is not None:
                inc.c += n
                self.ops[eng].append(
                    lambda e, inc=inc, n=n: e.drain().then_inc(inc.h, n))
            else:
                self.ops[eng].append(lambda e: e.drain())
        elif inc is not None:
            inc.c += n

            def wrapped(e, fn=fn, inc=inc, n=n):
                fn(e).then_inc(inc.h, n)
            self.ops[eng].append(wrapped)
        else:
            self.ops[eng].append(fn)

    def wait(self, eng, sem, at=None):
        tgt = sem.c if at is None else at
        if tgt <= 0:
            return
        prev = self._waited[eng].get(sem, 0)
        if tgt <= prev:
            return
        self._waited[eng][sem] = tgt
        self.ops[eng].append(lambda e, s=sem, t=tgt: e.wait_ge(s.h, t))

    def wait_mark(self, eng, mark):
        if mark is not None:
            self.wait(eng, mark[0], mark[1])

    def build(self, stack):
        nc = self.nc
        for s in self.sems:
            s.h = stack.enter_context(nc.semaphore(s.name))
        block = stack.enter_context(nc.Block())
        for eng_name in ENGINES:
            lst = self.ops[eng_name]
            if not lst:
                continue

            def section(engine, lst=lst):
                for f in lst:
                    f(engine)
            getattr(block, eng_name)(section)


def build_nc():
    nc = bass.Bass("TRN2", target_bir_lowering=False, debug=False)

    xin = nc.dram_tensor("x", [T, D], F32, kind="ExternalInput").ap()
    out = nc.dram_tensor("out", [T, D], F32, kind="ExternalOutput").ap()

    def din(name, shape):
        return nc.dram_tensor(name, shape, F32, kind="ExternalInput").ap()

    n1w = din("norm1_w", [D]); n2w = din("norm2_w", [D])
    A_log = din("A_log", [N]); D_param = din("D_param", [D]); p_scale = din("scale", [D])
    Wdt = din("Wdt", [D, D]); bdt = din("bdt", [D])
    WB = din("WB", [N, D]); bB = din("bB", [N])
    WC = din("WC", [N, D]); bC = din("bC", [N])
    W1 = din("W1", [DFF, D]); b1 = din("b1", [DFF])
    W2 = din("W2", [D, DFF]); b2 = din("b2", [D])
    Wr = din("Wr", [D, E]); br = din("br", [E])
    Wg_s = din("Wg_s", [D, I]); bg_s = din("bg_s", [I])
    Wu_s = din("Wu_s", [D, I]); bu_s = din("bu_s", [I])
    Wd_s = din("Wd_s", [I, D]); bd_s = din("bd_s", [D])
    Wg = din("Wg", [E, D, I]); bg = din("bg", [E, I])
    Wu = din("Wu", [E, D, I]); bu = din("bu", [E, I])
    Wd = din("Wd", [E, I, D]); bd = din("bd", [E, D])

    Wdt_b = nc.dram_tensor("Wdt_b", [D, D], BF16, kind="Internal").ap()
    W1_b = nc.dram_tensor("W1_b", [DFF, D], BF16, kind="Internal").ap()
    W2_b = nc.dram_tensor("W2_b", [D, DFF], BF16, kind="Internal").ap()
    WdtT_b = nc.dram_tensor("WdtT_b", [D, D], BF16, kind="Internal").ap()
    W1T_b = nc.dram_tensor("W1T_b", [D, DFF], BF16, kind="Internal").ap()
    W2T_b = nc.dram_tensor("W2T_b", [DFF, D], BF16, kind="Internal").ap()

    pr = Prog(nc)
    dsm = pr.sem("dsm")
    dmg = pr.sem("dmg")
    dx = pr.sem("dx")
    dwdt = [pr.sem(f"dwdt{i}") for i in range(2)]
    dq = [pr.sem(f"dq{i}") for i in range(2)]
    dgu = [pr.sem(f"dgu{i}") for i in range(4)]
    dwd = [pr.sem(f"dwd{i}") for i in range(2)]
    dpl = [pr.sem(f"dpl{i}") for i in range(2)]
    dps = [pr.sem(f"dps{i}") for i in range(2)]
    dms = pr.sem("dms")
    st = pr.sem("st")
    sv = pr.sem("sv")
    ss = pr.sem("ss")
    sg = pr.sem("sg")

    stack = contextlib.ExitStack()
    Ecm = stack.enter_context

    def sb(name, shape, dt=F32):
        return Ecm(nc.sbuf_tensor(name, shape, dt))

    acc = sb("acc", [128, (G // 128) * D])
    xnT_bf = sb("xnT_bf", [128, NKD * G], BF16)
    xn3T_bf = sb("xn3T_bf", [128, NKD * G], BF16)
    w_sb = sb("w_sb", [128, (G // 128) * E])
    wT_sb = sb("wT_sb", [E, G], BF16)
    Wr_sb = sb("Wr_sb", [128, NKD * E])
    WBT = sb("WBT", [128, NKD * N], BF16); WCT = sb("WCT", [128, NKD * N], BF16)
    bd_sb = sb("bd_sb", [E, D], BF16)
    b2_r = sb("b2_r", [1, D], BF16); bds_r = sb("bds_r", [1, D], BF16)
    ones_b = sb("ones_b", [1, 512], BF16)
    bB_r = sb("bB_r", [1, N]); bC_r = sb("bC_r", [1, N]); br_r = sb("br_r", [1, E])
    n1w_c = sb("n1w_c", [128, NKD]); n2w_c = sb("n2w_c", [128, NKD])
    Dp_c = sb("Dp_c", [128, NKD]); sc_c = sb("sc_c", [128, NKD])
    bdt_c = sb("bdt_c", [128, NKD])
    b1_c = sb("b1_c", [128, DFF // 128])
    bg_c = sb("bg_c", [128, E * 4]); bu_c = sb("bu_c", [128, E * 4])
    bgs_c = sb("bgs_c", [128, 4]); bus_c = sb("bus_c", [128, 4])
    A_bc = sb("A_bc", [128, N])
    recipA = sb("recipA", [N, 1])
    ones_r = sb("ones_r", [1, 512])
    ones_c = sb("ones_c", [128, 1])
    ident = sb("ident", [128, 128], F32)
    A_lr = sb("A_lr", [1, N]); A_lc = sb("A_lc", [N, 1])
    tmpA = sb("tmpA", [1, N]); tmpA2 = sb("tmpA2", [N, 1])
    stat = sb("stat", [128, 8])
    aff = sb("aff", [128, E]); top8 = sb("top8", [128, 8]); work = sb("work", [128, E])
    maskb = sb("maskb", [128, E]); wsel = sb("wsel", [128, E])
    ssumc = sb("ssumc", [128, 1]); sinvc = sb("sinvc", [128, 1])
    Bv_sb = sb("Bv_sb", [N, CH], BF16); Cv_sb = sb("Cv_sb", [N, CH], BF16)
    r2s = sb("r2s", [1, CH])
    state = sb("state", [128, 128])
    xnf = sb("xnf", [128, CH])
    rp_sb = sb("rp_sb", [128, 2 * E])
    bsil = sb("bsil", [128, 512], BF16); btmp = sb("btmp", [128, 512], BF16)
    wb_sb = sb("wb_sb", [128, 1024], BF16)
    u_ring = sb("u_ring", [128, 4 * 512], BF16)
    constv = sb("constv", [128, 4])
    sq_big = sb("sq_big", [128, D])
    sel32 = sb("sel32", [E, E * 128], BF16)

    AR = 51200
    arena = sb("arena", [128, AR], BF16)
    x2T_b = arena[:, 0:4096]
    dt_b = arena[:, 4096:8192]
    xr_b = arena[:, 8192:12288]
    ya = arena[:, 12288:20480].bitcast(F32)
    xn_s = arena[:, 20480:22528].bitcast(F32)
    dA0 = arena[:, 22528:23040]
    dA1 = arena[:, 23040:23552]
    q0_sb = arena[:, 23552:24064]
    q1_sb = arena[:, 24064:24576]
    h0_sb = arena[:, 24576:25088]
    h1_sb = arena[:, 25088:25600]
    bt0_sb = arena[:, 25600:26112]
    bt1_sb = arena[:, 26112:26624]
    sp_a = arena[:, 26624:27648].bitcast(F32)
    sp_b = arena[:, 27648:28672].bitcast(F32)
    sp_c = arena[:, 28672:29696].bitcast(F32)
    sqt = sp_c
    xnr_h = sp_a
    WdtT0 = arena[:, 29696:30720]
    WdtT1 = WdtT0
    q_sb = sp_c
    WGU0 = 30720
    WD0 = 38912
    V0 = 43008

    def wgu_slot(k):
        b = WGU0 + (k % 4) * 2048
        return arena[:, b:b + 1024], arena[:, b + 1024:b + 2048]

    def wd_slot(k):
        b = WD0 + (k % 2) * 2048
        return arena[:, b:b + 2048]

    def v_slot(k):
        b = V0 + (k % 2) * 4096
        return arena[:, b:b + 4096]

    def mlpq(p):
        base = 30720 + p * 8192
        return arena[:, base:base + 4096], arena[:, base + 4096:base + 8192]

    hTm = arena[:, 47104:51200]

    tmp_w_a = xn3T_bf[:, 0:4096]
    tmp_w_b = xn3T_bf[:, 4096:8192]

    pb = [Ecm(nc.psum_tensor(f"pb{i}", [128, 512], F32)) for i in range(8)]

    ready = {}
    freed = {}

    def mark(d, key, sem):
        d[key] = (sem, sem.c)

    def need(eng, d, key):
        if key in d:
            pr.wait(eng, d[key][0], d[key][1])

    def bank_free(i, sem):
        mark(freed, f"pb{i}", sem)

    def need_bank(i):
        need("tensor", freed, f"pb{i}")

    def mm(o, l, r, start, stop, inc=False):
        pr.on("tensor", lambda e, o=o, l=l, r=r, s=start, t=stop:
              e.matmul(o, l, r, start=s, stop=t), inc=(st if inc else None))

    def tr(o, a, inc=False):
        pr.on("tensor", lambda e, o=o, a=a: e.transpose(o, a, ident[:, :]),
              inc=(st if inc else None))

    def v_tt(o, a, b, op, inc=False, nd=False):
        pr.on("vector", lambda e, o=o, a=a, b=b, op=op: e.tensor_tensor(
            out=o, in0=a, in1=b, op=op), inc=(sv if inc else None), drain=not nd)

    def v_ts(o, a, s1, s2, op0, op1=None, inc=False):
        if op1 is None:
            pr.on("vector", lambda e, o=o, a=a, s1=s1, op0=op0: e.tensor_scalar(
                out=o, in0=a, scalar1=s1, scalar2=None, op0=op0),
                inc=(sv if inc else None), drain=True)
        else:
            pr.on("vector", lambda e, o=o, a=a, s1=s1, s2=s2, op0=op0, op1=op1:
                  e.tensor_scalar(out=o, in0=a, scalar1=s1, scalar2=s2, op0=op0,
                                  op1=op1), inc=(sv if inc else None), drain=True)

    def v_stt(o, a, s, b, op0, op1, inc=False, nd=False):
        pr.on("vector", lambda e, o=o, a=a, s=s, b=b, op0=op0, op1=op1:
              e.scalar_tensor_tensor(out=o, in0=a, scalar=s, in1=b, op0=op0,
                                     op1=op1), inc=(sv if inc else None), drain=not nd)

    def v_copy(o, a, inc=False):
        pr.on("vector", lambda e, o=o, a=a: e.tensor_copy(out=o, in_=a),
              inc=(sv if inc else None), drain=True)

    def v_recip(o, a, inc=False):
        pr.on("vector", lambda e, o=o, a=a: e.reciprocal(out=o, in_=a),
              inc=(sv if inc else None), drain=True)

    def v_red(o, a, op, inc=False):
        pr.on("vector", lambda e, o=o, a=a, op=op: e.tensor_reduce(
            out=o, in_=a, axis=mybir.AxisListType.X, op=op),
            inc=(sv if inc else None), drain=True)

    def s_act(o, a, f, bias=None, scale=None, accum=None, inc=True, nd=False):
        kw = {}
        if bias is not None:
            kw["bias"] = bias
        if scale is not None:
            kw["scale"] = scale
        if accum is not None:
            kw["accum_out"] = accum
        pr.on("scalar", lambda e, o=o, a=a, f=f, kw=kw: e.activation(
            out=o, in_=a, func=f, **kw), inc=(ss if inc else None), drain=not nd)

    def v_drain(inc=False):
        if inc:
            sv.c += 1
            pr.ops["vector"].append(lambda e: e.drain().then_inc(sv.h, 1))
        else:
            pr.ops["vector"].append(lambda e: e.drain())

    def s_drain(inc=False):
        if inc:
            ss.c += 1
            pr.ops["scalar"].append(lambda e: e.drain().then_inc(ss.h, 1))
        else:
            pr.ops["scalar"].append(lambda e: e.drain())

    def s_copy(o, a, inc=True, nd=False):
        pr.on("scalar", lambda e, o=o, a=a: e.copy(out=o, in_=a),
              inc=(ss if inc else None), drain=not nd)

    def fence(*engs):
        sems = {"tensor": st, "vector": sv, "scalar": ss, "gpsimd": sg}
        marks = {}
        for eng in engs:
            s = sems[eng]
            s.c += 1
            pr.ops[eng].append(lambda e, s=s: e.drain().then_inc(s.h, 1))
            marks[eng] = (s, s.c)
        return marks

    def wait_fence(eng, marks):
        if marks:
            for m_ in marks.values():
                pr.wait_mark(eng, m_)

    def L(dst, src, slow=False):
        pr.on("sync", lambda e, d=dst, s=src, sl=slow: e.dma_start(
            out=d, in_=s, allow_slow_non_contiguous=sl), inc=dsm, n=16)

    def Lg(dst, src, slow=False):
        pr.on("gpsimd", lambda e, d=dst, s=src, sl=slow: e.dma_start(
            out=d, in_=s, allow_slow_non_contiguous=sl), inc=dmg, n=16)

    L(n1w_c[:, :], n1w.rearrange("(c p) -> p c", p=128), True)
    L(n2w_c[:, :], n2w.rearrange("(c p) -> p c", p=128), True)
    L(Dp_c[:, :], D_param.rearrange("(c p) -> p c", p=128), True)
    L(sc_c[:, :], p_scale.rearrange("(c p) -> p c", p=128), True)
    L(bdt_c[:, :], bdt.rearrange("(c p) -> p c", p=128), True)
    L(b1_c[:, :], b1.rearrange("(c p) -> p c", p=128), True)
    for e_ in range(E):
        L(bg_c[:, e_ * 4:(e_ + 1) * 4], bg[e_].rearrange("(c p) -> p c", p=128), True)
        L(bu_c[:, e_ * 4:(e_ + 1) * 4], bu[e_].rearrange("(c p) -> p c", p=128), True)
    L(bgs_c[:, :], bg_s.rearrange("(c p) -> p c", p=128), True)
    L(bus_c[:, :], bu_s.rearrange("(c p) -> p c", p=128), True)
    L(A_lr[:, :], A_log[None, :])
    L(A_lc[:, :], A_log[:, None])
    for c_ in range(NKD):
        L(Wr_sb[:, c_ * E:(c_ + 1) * E], Wr[c_ * 128:(c_ + 1) * 128, :])
    L(bB_r[:, :], bB[None, :]); L(bC_r[:, :], bC[None, :]); L(br_r[:, :], br[None, :])
    small_loads = (dsm, dsm.c)
    Lg(bd_sb[:, :], bd)
    Lg(b2_r[:, :], b2[None, :]); Lg(bds_r[:, :], bd_s[None, :])
    for c_ in range(NKD):
        Lg(WBT[:, c_ * N:(c_ + 1) * N],
           WB[:, c_ * 128:(c_ + 1) * 128].rearrange("n p -> p n"), True)
        Lg(WCT[:, c_ * N:(c_ + 1) * N],
           WC[:, c_ * 128:(c_ + 1) * 128].rearrange("n p -> p n"), True)
    pr.on("gpsimd", lambda e: e.dma_start(out=Wdt_b, in_=Wdt), inc=dmg, n=16)
    pr.on("gpsimd", lambda e: e.dma_start(out=W1_b, in_=W1), inc=dmg, n=16)
    pr.on("gpsimd", lambda e: e.dma_start(out=W2_b, in_=W2), inc=dmg, n=16)
    casts_done = (dmg, dmg.c)
    wbct_loads = (dmg, dmg.c)

    for f_ in [lambda e: e.memset(constv[:, 0:1], 1e-5),
               lambda e: e.memset(constv[:, 1:2], 1e-6),
               lambda e: e.memset(constv[:, 2:3], 1.0 / D),
               lambda e: e.memset(ones_r[:, :], 1.0),
               lambda e: e.memset(ones_b[:, :], 1.0),
               lambda e: e.memset(ones_c[:, :], 1.0),
               lambda e: e.memset(state[:, :], 0.0),
               lambda e: e.memset(ident[:, :], 0.0),
               lambda e: e.memset(sel32[:, :], 1.0),
               lambda e: e.affine_select(
                   out=sel32[:, :], in_=sel32[:, :], compare_op=OP.is_ge, fill=0.0,
                   base=0, pattern=[[1, E * 128]], channel_multiplier=-128),
               lambda e: e.affine_select(
                   out=sel32[:, :], in_=sel32[:, :], compare_op=OP.is_ge, fill=0.0,
                   base=127, pattern=[[-1, E * 128]], channel_multiplier=128)]:
        pr.on("gpsimd", f_, drain=True)
    pr.on("gpsimd", lambda e: e.affine_select(
        out=ident[:, :], in_=ident[:, :], compare_op=OP.not_equal,
        fill=1.0, base=0, pattern=[[-1, 128]], channel_multiplier=1),
        inc=sg, drain=True)
    consts_ready = (sg, sg.c)

    pr.wait_mark("scalar", small_loads)
    s_act(tmpA[:, :], A_lr[:, :], AF.Exp)
    s_act(tmpA2[:, :], A_lc[:, :], AF.Exp, scale=-1.0)
    pr.wait("vector", ss, ss.c)
    v_ts(tmpA[:, :], tmpA[:, :], -1.0, None, OP.mult, inc=True)
    v_ts(recipA[:, :], tmpA2[:, :], -1.0, None, OP.mult, inc=True)
    A_row_ready = (sv, sv.c)
    pr.wait_mark("tensor", consts_ready)
    pr.wait_mark("tensor", A_row_ready)
    mm(pb[0][:, 0:N], ones_r[0:1, 0:128], tmpA[0:1, :], True, True, inc=True)
    pr.wait("vector", st, st.c)
    v_copy(A_bc[:, :], pb[0][:, 0:N], inc=True)
    bank_free(0, sv)

    pr.wait_mark("sync", casts_done)

    glb = {"blk": 0}

    def prep_transpose(src, dst, nblk, blkrows):
        for b_ in range(nblk):
            par = glb["blk"] % 2
            glb["blk"] += 1
            buf = (tmp_w_a if par == 0 else tmp_w_b)[:, 0:blkrows]
            pr.wait("sync", dps[par], dps[par].c)
            pr.on("sync", lambda e, o=buf, s=src, b_=b_: e.dma_start_transpose(
                out=o, in_=s[:, b_ * 128:(b_ + 1) * 128]), inc=dpl[par], n=16)
            pr.wait("sync", dpl[par], dpl[par].c)
            pr.on("sync", lambda e, o=dst, i=buf, b_=b_: e.dma_start(
                out=o[b_ * 128:(b_ + 1) * 128, :], in_=i), inc=dps[par], n=16)

    prep_transpose(Wdt_b, WdtT_b, NKD, D)
    prep_transpose(W1_b, W1T_b, NKD, DFF)
    prep_transpose(W2_b, W2T_b, DFF // 128, D)
    prep_stores = [(dps[0], dps[0].c), (dps[1], dps[1].c)]
    freed["xn3T_bf"] = prep_stores[0]
    freed["xn3T_bf2"] = prep_stores[1]

    nE = E + 1
    pairs = [list(range(p_, min(p_ + 2, nE))) for p_ in range(0, nE, 2)]

    def make_B_units(g):
        units = []
        gu_idx = {"n": 0}

        def load_gu(k):
            e_, it = divmod(k, 4)
            if e_ >= nE:
                return
            wgt, wut = wgu_slot(k)
            sgu = dgu[k % 4]
            sg_, su_ = ((Wg[e_], Wu[e_]) if e_ < E else (Wg_s, Wu_s))

            def em():
                need("gpsimd", freed, f"gu{k % 4}")
                pr.on("gpsimd", lambda e, o=wgt, s=sg_, it=it: e.dma_start(
                    out=o.rearrange("p (hc i2) -> p hc i2", i2=128),
                    in_=s.rearrange("(hc p) i -> p hc i", p=128)[
                        :, :, it * 128:(it + 1) * 128],
                    allow_slow_non_contiguous=True), inc=sgu, n=16)
                pr.on("gpsimd", lambda e, o=wut, s=su_, it=it: e.dma_start(
                    out=o.rearrange("p (hc i2) -> p hc i2", i2=128),
                    in_=s.rearrange("(hc p) i -> p hc i", p=128)[
                        :, :, it * 128:(it + 1) * 128],
                    allow_slow_non_contiguous=True), inc=sgu, n=16)
                ready[f"gur{k % 4}"] = (sgu, sgu.c)
            return em

        def load_wd_e(e_, dh):
            wdt_ = wd_slot(e_)
            sdd = dwd[e_ % 2]
            src_ = Wd[e_] if e_ < E else Wd_s

            def em():
                need("gpsimd", freed, f"wd{e_ % 2}")
                pr.on("gpsimd", lambda e, o=wdt_, s=src_, dh=dh: e.dma_start(
                    out=o.rearrange("p (ic h2) -> p ic h2", h2=512),
                    in_=s.rearrange("(ic p) h -> p ic h", p=128)[
                        :, :, dh * 512:(dh + 1) * 512],
                    allow_slow_non_contiguous=True), inc=sdd, n=16)
                ready[f"wdr{e_ % 2}"] = (sdd, sdd.c)
            return em

        def gu_unit(e_, it):
            k = e_ * 4 + it
            wgt, wut = wgu_slot(k)

            def em():
                pr.wait_mark("tensor", ready.get(f"gur{k % 4}"))
                pr.wait_mark("tensor", ready["xnT_g"])
                if e_ < E and it == 0:
                    pr.wait_mark("tensor", ready["w_g"])
                    need_bank(6); need_bank(7)
                    mm(pb[6][:, :], sel32[0:E, e_ * 128:(e_ + 1) * 128],
                       wT_sb[0:E, 0:512], True, True)
                    mm(pb[7][:, :], sel32[0:E, e_ * 128:(e_ + 1) * 128],
                       wT_sb[0:E, 512:1024], True, True, inc=True)
                    wbm = st.c
                    pr.wait("scalar", st, wbm)
                    need("scalar", freed, "wb_sb")
                    s_copy(wb_sb[:, 0:512], pb[6][:, :], inc=False, nd=True)
                    s_copy(wb_sb[:, 512:1024], pb[7][:, :])
                    bank_free(6, ss); bank_free(7, ss)
                    ready["wb"] = (ss, ss.c)
                gcol = (bg_c[:, e_ * 4 + it:e_ * 4 + it + 1] if e_ < E
                        else bgs_c[:, it:it + 1])
                ucol = (bu_c[:, e_ * 4 + it:e_ * 4 + it + 1] if e_ < E
                        else bus_c[:, it:it + 1])
                up = (k % 2) * 1024
                u0 = u_ring[:, up:up + 512]
                u1 = u_ring[:, up + 512:up + 1024]
                st_marks = []
                for src_w, bnk0 in ((wgt, 2), (wut, 4)):
                    for th in range(2):
                        need_bank(bnk0 + th)
                        for hc in range(NKD):
                            mm(pb[bnk0 + th][:, :], src_w[:, hc * 128:(hc + 1) * 128],
                               xnT_bf[:, hc * G + th * 512:hc * G + (th + 1) * 512],
                               hc == 0, hc == NKD - 1, inc=(hc == NKD - 1))
                        st_marks.append(st.c)
                need("scalar", freed, "bsil")
                need("scalar", freed, f"uring{k % 2}")
                pr.wait("scalar", st, st_marks[0])
                s_act(bsil[:, :], pb[2][:, :], AF.Silu, bias=gcol, inc=False, nd=True)
                pr.wait("scalar", st, st_marks[1])
                s_act(btmp[:, :], pb[3][:, :], AF.Silu, bias=gcol, inc=True)
                bank_free(2, ss); bank_free(3, ss)
                pr.wait("scalar", st, st_marks[2])
                s_copy(u0, pb[4][:, :], inc=False, nd=True)
                pr.wait("scalar", st, st_marks[3])
                s_copy(u1, pb[5][:, :], inc=True)
                bank_free(4, ss); bank_free(5, ss)
                pr.wait("vector", ss, ss.c)
                if e_ < E:
                    pr.wait_mark("vector", ready["wb"])
                need("vector", freed, f"v{e_ % 2}")
                vt0 = v_slot(e_)[:, it * 1024: it * 1024 + 512]
                vt1 = v_slot(e_)[:, it * 1024 + 512: it * 1024 + 1024]
                if e_ < E:
                    v_tt(bsil[:, :], bsil[:, :], wb_sb[:, 0:512], OP.mult, nd=True)
                    v_tt(btmp[:, :], btmp[:, :], wb_sb[:, 512:1024], OP.mult)
                v_stt(vt0, u0, ucol, bsil[:, :], OP.add, OP.mult, nd=True)
                v_stt(vt1, u1, ucol, btmp[:, :], OP.add, OP.mult, inc=True)
                mark(freed, "bsil", sv)
                mark(freed, f"uring{k % 2}", sv)
                if e_ < E and it == 3:
                    mark(freed, "wb_sb", sv)
                if it == 3:
                    ready[f"vr{e_ % 2}"] = (sv, sv.c)
                mark(freed, f"gu{k % 4}", sv)
                nxt = load_gu(k + 4)
                if nxt is not None:
                    nxt()
            return em

        def down_unit(pi, pair, dh, t_, dpar):
            def em():
                for e_ in pair:
                    pr.wait_mark("tensor", ready.get(f"vr{e_ % 2}"))
                    pr.wait_mark("tensor", ready.get(f"wdr{e_ % 2}"))
                th, tq = t_ // 4, t_ % 4
                bnk = 6 + dpar
                need_bank(bnk)
                first = True
                if pi == 0:
                    pr.wait_mark("tensor", ready["w_g"])
                    mm(pb[bnk][:, :],
                       wT_sb[0:E, th * 512 + tq * 128: th * 512 + (tq + 1) * 128],
                       bd_sb[0:E, dh * 512:(dh + 1) * 512], True, False)
                    first = False
                if nE - 1 in pair:
                    mm(pb[bnk][:, :], ones_b[0:1, 0:128],
                       bds_r[0:1, dh * 512:(dh + 1) * 512], first, False)
                    first = False
                for e_ in pair:
                    wdt_ = wd_slot(e_)
                    for it in range(4):
                        vt = v_slot(e_)[:, it * 1024 + th * 512 + tq * 128:
                                        it * 1024 + th * 512 + (tq + 1) * 128]
                        last = (e_ == pair[-1]) and (it == 3)
                        mm(pb[bnk][:, :], vt,
                           wdt_[:, it * 512:(it + 1) * 512],
                           first, last, inc=last)
                        first = False
                pr.wait("vector", st, st.c)
                av = acc[:, t_ * D + dh * 512: t_ * D + (dh + 1) * 512]
                need("vector", freed, "acc_v")
                v_tt(av, av, pb[bnk][:, :], OP.add, inc=True)
                bank_free(bnk, sv)
                if t_ == 7:
                    for e_ in pair:
                        mark(freed, f"wd{e_ % 2}", st)
                        if dh == 1:
                            mark(freed, f"v{e_ % 2}", st)
            return em

        dpar = {"p": 0}
        for pi, pair in enumerate(pairs):
            for e_ in pair:
                units.append((load_wd_e(e_, 0), 0.3))
            for e_ in pair:
                for it in range(4):
                    units.append((gu_unit(e_, it), 7.0))
            for dh in range(2):
                if dh == 1:
                    for e_ in pair:
                        units.append((load_wd_e(e_, 1), 0.3))
                for t_ in range(8):
                    units.append((down_unit(pi, pair, dh, t_, dpar["p"]), 2.4))
                    dpar["p"] = 1 - dpar["p"]
        return units, load_gu

    markMLP = None
    for g in range(NG):
        need("sync", freed, "acc")
        pr.on("sync", lambda e, g=g: e.dma_start(
            out=acc[:, :].rearrange("p (t d) -> p t d", d=D),
            in_=xin[g * G:(g + 1) * G, :].rearrange("(t p) d -> p t d", p=128)),
            inc=dx, n=16)
        x_ready = (dx, dx.c)
        pr.wait_mark("vector", small_loads)
        pr.wait_mark("scalar", small_loads)
        pr.wait_mark("vector", consts_ready)
        pr.wait_mark("scalar", consts_ready)

        B_units, load_gu = make_B_units(g)
        bu_iter = {"i": 0}

        def bslot(budget_us=18.0):
            used = 0.0
            while used < budget_us:
                i = bu_iter["i"]
                if i >= len(B_units):
                    return
                em, cost = B_units[i]
                bu_iter["i"] = i + 1
                em()
                used += cost

        def bflush():
            while bu_iter["i"] < len(B_units):
                B_units[bu_iter["i"]][0]()
                bu_iter["i"] += 1

        wait_fence("gpsimd", markMLP)
        for k in range(4):
            f_ = load_gu(k)
            if f_ is not None:
                f_()

        def emit_A12(ch2):
            tau0 = ch2 * NTAU
            need("vector", freed, "xn_s"); need("scalar", freed, "xn_s")
            need("scalar", freed, "x2T_b")
            need("scalar", freed, "xnT_bf")
            for t_ in range(NTAU):
                xv = acc[:, (tau0 + t_) * D:(tau0 + t_ + 1) * D]
                c0 = stat[:, 0:1]; c1 = stat[:, 1:2]; cx = stat[:, 2:3]
                c3 = stat[:, 3:4]; c4 = stat[:, 4:5]; c5 = stat[:, 5:6]
                r1c = stat[:, 6:7]
                pr.wait_mark("vector", x_ready)
                pr.wait_mark("scalar", x_ready)
                v_red(c0, xv, OP.add)
                s_act(sq_big[:, :], xv, AF.Square, accum=c1)
                pr.wait("vector", ss, ss.c)
                v_ts(c0, c0, 1.0 / D, None, OP.mult)
                v_ts(cx, c1, 1.0 / D, None, OP.mult)
                v_tt(c3, c0, c0, OP.mult)
                v_tt(c3, cx, c3, OP.subtract, inc=True)
                pr.wait("scalar", sv, sv.c)
                s_act(c4, c3, AF.Sqrt, bias=constv[:, 0:1])
                pr.wait("vector", ss, ss.c)
                v_recip(c5, c4)
                need("vector", freed, "xn_s")
                v_ts(xn_s[:, :], xv, c0, c5, OP.subtract, OP.mult, inc=True)
                pr.wait("scalar", sv, sv.c)
                s_act(sq_big[:, :], xn_s[:, :], AF.Square, accum=c1)
                s_act(c4, c1, AF.Sqrt, scale=constv[:, 2:3], bias=constv[:, 1:2])
                pr.wait("vector", ss, ss.c)
                v_recip(r1c, c4, inc=True)
                xn_t_ready = (sv, sv.c)

                pr.wait_mark("tensor", xn_t_ready)
                pr.wait_mark("tensor", consts_ready)
                for half in range(2):
                    need("vector", freed, "xnr_h")
                    v_ts(xnr_h[:, :], xn_s[:, half * 512:(half + 1) * 512],
                         r1c, None, OP.mult, inc=True)
                    need_bank(0); need_bank(1)
                    pr.wait("tensor", sv, sv.c)
                    for k4 in range(4):
                        kd = half * 4 + k4
                        tr(pb[0][:, k4 * 128:(k4 + 1) * 128],
                           xn_s[:, kd * 128:(kd + 1) * 128])
                    for k4 in range(4):
                        tr(pb[1][:, k4 * 128:(k4 + 1) * 128],
                           xnr_h[:, k4 * 128:(k4 + 1) * 128],
                           inc=(k4 == 3))
                    tmark = st.c
                    mark(freed, "xnr_h", st)
                    if half == 1:
                        mark(freed, "xn_s", st)
                    pr.wait("scalar", st, tmark)
                    xnT_v = xnT_bf[:, :].rearrange("p (kd t2) -> p kd t2", t2=G)[
                        :, half * 4:(half + 1) * 4,
                        ch2 * CH + t_ * 128: ch2 * CH + (t_ + 1) * 128]
                    s_copy(xnT_v, pb[0][:, :], inc=False, nd=True)
                    for k4 in range(4):
                        kd = half * 4 + k4
                        s_act(x2T_b[:, kd * CH + t_ * 128: kd * CH + (t_ + 1) * 128],
                              pb[1][:, k4 * 128:(k4 + 1) * 128], AF.Copy,
                              scale=n1w_c[:, kd:kd + 1], nd=(k4 < 3),
                              inc=(k4 == 3))
                    bank_free(1, ss)
                    pr.wait("vector", ss, ss.c)
                    need("vector", freed, "xnf")
                    v_copy(xnf[:, :], pb[0][:, :], inc=True)
                    bank_free(0, sv)
                    need_bank(0)
                    pr.wait("tensor", sv, sv.c)
                    if half == 0:
                        mm(pb[0][0:128, 0:E], ones_r[0:1, 0:128], br_r[0:1, :],
                           True, False)
                    for k4 in range(4):
                        kd = half * 4 + k4
                        mm(pb[0][0:128, 0:E], xnf[:, k4 * 128:(k4 + 1) * 128],
                           Wr_sb[:, kd * E:(kd + 1) * E], half == 1 and k4 == 0,
                           k4 == 3, inc=(k4 == 3))
                    mark(freed, "xnf", st)
                    pr.wait("scalar", st, st.c)
                    need("scalar", freed, "rp")
                    s_copy(rp_sb[:, half * E:(half + 1) * E], pb[0][0:128, 0:E])
                    bank_free(0, ss)
                pr.wait("vector", ss, ss.c)
                v_tt(rp_sb[:, 0:E], rp_sb[:, 0:E], rp_sb[:, E:2 * E], OP.add, inc=True)
                mark(freed, "rp", sv)
                pr.wait("scalar", sv, sv.c)
                need("scalar", freed, "aff")
                s_act(aff[:, :], rp_sb[:, 0:E], AF.Sigmoid)
                pr.wait("vector", ss, ss.c)
                pr.on("vector", lambda e: e.max(out=top8[:, :], in_=aff[:, :]), drain=True)
                pr.on("vector", lambda e: e.match_replace(
                    out=work[:, :], in_to_replace=top8[:, :], in_values=aff[:, :],
                    imm_value=-1.0), drain=True)
                v_ts(maskb[:, :], work[:, :], -1.0, None, OP.is_equal)
                v_tt(wsel[:, :], aff[:, :], maskb[:, :], OP.mult, inc=True)
                mark(freed, "aff", sv)
                v_red(ssumc[:, 0:1], top8[:, :], OP.add)
                v_ts(ssumc[:, 0:1], ssumc[:, 0:1], 1e-9, None, OP.add)
                v_recip(sinvc[:, 0:1], ssumc[:, 0:1])
                wv = w_sb[:, (tau0 + t_) * E:(tau0 + t_ + 1) * E]
                need("vector", freed, "w_sb")
                v_ts(wv, wsel[:, :], sinvc[:, 0:1], None, OP.mult, inc=True)

            need_bank(0)
            pr.wait("tensor", sv, sv.c)
            for t_ in range(NTAU):
                tr(pb[0][0:E, t_ * 128:(t_ + 1) * 128],
                   w_sb[:, (tau0 + t_) * E:(tau0 + t_ + 1) * E],
                   inc=(t_ == NTAU - 1))
            pr.wait("vector", st, st.c)
            need("vector", freed, "wT_sb")
            v_copy(wT_sb[:, ch2 * CH:(ch2 + 1) * CH], pb[0][0:E, 0:CH], inc=True)
            bank_free(0, sv)
            if ch2 == 1:
                ready["w_g"] = (sv, sv.c)
                ready["xnT_g"] = (ss, ss.c)

        def emit_dtbv(ch2, with_b):
            x2T_ready = (ss, ss.c)
            pr.wait_mark("tensor", x2T_ready)
            need("vector", freed, "dt_b"); need("vector", freed, "xr_b")
            need("vector", freed, "sqt")
            need("scalar", freed, "xnr_h")
            for kd in range(NKD):
                wslice = WdtT0
                sw = dwdt[0]
                need("sync", freed, "wdt")
                if g == 0 and ch2 == 0 and kd < 2:
                    for m_ in prep_stores:
                        pr.wait_mark("sync", m_)
                pr.on("sync", lambda e, o=wslice, kd=kd: e.dma_start(
                    out=o.rearrange("p (c d2) -> p c d2", d2=128),
                    in_=WdtT_b.rearrange("(c p) d2 -> p c d2", p=128)[
                        :, :, kd * 128:(kd + 1) * 128],
                    allow_slow_non_contiguous=True), inc=sw, n=16)
                pr.wait_mark("tensor", (sw, sw.c))
                bnk = kd % 2
                need_bank(bnk)
                for ci in range(NKD):
                    mm(pb[bnk][:, :], wslice[:, ci * 128:(ci + 1) * 128],
                       x2T_b[:, ci * CH:(ci + 1) * CH], ci == 0, ci == NKD - 1,
                       inc=(ci == NKD - 1))
                mark(freed, "wdt", st)
                pr.wait("scalar", st, st.c)
                need("scalar", freed, "spbc")
                s_act(sp_a[:, :], pb[bnk][:, :], AF.Abs, bias=bdt_c[:, kd:kd + 1],
                      inc=False)
                s_act(sp_b[:, :], pb[bnk][:, :], AF.Relu, bias=bdt_c[:, kd:kd + 1],
                      inc=True)
                bank_free(bnk, ss)
                s_act(sp_a[:, :], sp_a[:, :], AF.Exp, scale=-1.0, inc=False)
                s_act(sp_c[:, :], sp_a[:, :], AF.Ln, bias=1.0, inc=True)
                pr.wait("vector", ss, ss.c)
                dsl = dt_b[:, kd * CH:(kd + 1) * CH]
                v_tt(dsl, sp_b[:, :], sp_c[:, :], OP.add)
                v_recip(q_sb[:, 0:CH], dsl)
                v_tt(xr_b[:, kd * CH:(kd + 1) * CH], q_sb[:, 0:CH],
                     x2T_b[:, kd * CH:(kd + 1) * CH], OP.mult, inc=True)
                mark(freed, "spbc", sv)
                if with_b and kd % 2 == 1:
                    bslot(6.0)
            dt_ready = (sv, sv.c)

            need_bank(0); need_bank(1)
            pr.wait_mark("tensor", wbct_loads)
            pr.wait_mark("tensor", dt_ready)
            for ci in range(NKD):
                mm(pb[0][0:N, 0:CH], WBT[:, ci * N:(ci + 1) * N],
                   x2T_b[:, ci * CH:(ci + 1) * CH], ci == 0, False)
            mm(pb[0][0:N, 0:CH], bB_r[0:1, :], ones_r[0:1, 0:CH], False, True, inc=True)
            for ci in range(NKD):
                mm(pb[1][0:N, 0:CH], WCT[:, ci * N:(ci + 1) * N],
                   x2T_b[:, ci * CH:(ci + 1) * CH], ci == 0, False)
            mm(pb[1][0:N, 0:CH], bC_r[0:1, :], ones_r[0:1, 0:CH], False, True, inc=True)
            bcv = st.c
            pr.wait("scalar", st, bcv)
            need("scalar", freed, "BvCv")
            s_copy(Bv_sb[:, :], pb[0][0:N, 0:CH])
            bank_free(0, ss)
            pr.wait("vector", st, bcv)
            pr.on("vector", lambda e: e.tensor_scalar(
                out=Cv_sb[:, :], in0=pb[1][0:N, 0:CH], scalar1=recipA[:, 0:1],
                scalar2=None, op0=OP.mult), inc=sv, drain=True)
            bank_free(1, sv)

            need("vector", freed, "ya")
            dcol_ap = bass.AP(Dp_c, 0, [[NKD, 128], [1, NKD], [0, CH]])
            pr.on("vector", lambda e, ap=dcol_ap: e.tensor_tensor(
                out=ya[:, :], in0=x2T_b[:, :], in1=ap, op=OP.mult), inc=sv, drain=True)
            return dt_ready, (ss, ss.c), (sv, sv.c)

        def emit_scan(ch2, dt_ready, bv_ready, cv_ready):
            pr.wait_mark("tensor", bv_ready)
            pr.wait_mark("tensor", cv_ready)
            need("vector", freed, "xnr_h")
            for n_ in range(N):
                need_bank(0); need_bank(1)
                mm(pb[0][:, :], sel32[0:N, n_ * 128:(n_ + 1) * 128],
                   Bv_sb[:, :], True, True)
                mm(pb[1][:, :], sel32[0:N, n_ * 128:(n_ + 1) * 128],
                   Cv_sb[:, :], True, True, inc=True)
                bc_ready = st.c
                pr.wait("vector", st, bc_ready)
                for kp in range(4):
                    kd0, kd1 = 2 * kp, 2 * kp + 1
                    need("scalar", freed, "dA")
                    pr.wait_mark("scalar", dt_ready)
                    s_act(dA0[:, :], dt_b[:, kd0 * CH:(kd0 + 1) * CH], AF.Exp,
                          scale=A_bc[:, n_:n_ + 1], inc=False, nd=True)
                    s_act(dA1[:, :], dt_b[:, kd1 * CH:(kd1 + 1) * CH], AF.Exp,
                          scale=A_bc[:, n_:n_ + 1], inc=True)
                    pr.wait("vector", ss, ss.c)
                    v_tt(q0_sb[:, :], pb[0][:, :], xr_b[:, kd0 * CH:(kd0 + 1) * CH],
                         OP.mult, nd=True)
                    v_tt(q1_sb[:, :], pb[0][:, :], xr_b[:, kd1 * CH:(kd1 + 1) * CH],
                         OP.mult)
                    v_stt(bt0_sb[:, :], dA0[:, :], -1.0, q0_sb[:, :], OP.add, OP.mult,
                          nd=True)
                    v_stt(bt1_sb[:, :], dA1[:, :], -1.0, q1_sb[:, :], OP.add, OP.mult)
                    col0 = state[:, n_ * 8 + kd0: n_ * 8 + kd0 + 1]
                    col1 = state[:, n_ * 8 + kd1: n_ * 8 + kd1 + 1]
                    pr.on("vector", lambda e, col=col0: e.tensor_tensor_scan(
                        out=h0_sb[:, :], data0=dA0[:, :], data1=bt0_sb[:, :],
                        initial=col, op0=OP.mult, op1=OP.add), drain=True)
                    pr.on("vector", lambda e, col=col1: e.tensor_tensor_scan(
                        out=h1_sb[:, :], data0=dA1[:, :], data1=bt1_sb[:, :],
                        initial=col, op0=OP.mult, op1=OP.add), inc=sv, drain=True)
                    mark(freed, "dA", sv)
                    pr.on("vector", lambda e, col=col0: e.tensor_copy(
                        out=col, in_=h0_sb[:, CH - 1:CH]))
                    pr.on("vector", lambda e, col=col1: e.tensor_copy(
                        out=col, in_=h1_sb[:, CH - 1:CH]))
                    v_tt(bt0_sb[:, :], h0_sb[:, :], pb[1][:, :], OP.mult, nd=True)
                    v_tt(bt1_sb[:, :], h1_sb[:, :], pb[1][:, :], OP.mult)
                    v_tt(ya[:, kd0 * CH:(kd0 + 1) * CH],
                         ya[:, kd0 * CH:(kd0 + 1) * CH], bt0_sb[:, :], OP.add, nd=True)
                    v_tt(ya[:, kd1 * CH:(kd1 + 1) * CH],
                         ya[:, kd1 * CH:(kd1 + 1) * CH], bt1_sb[:, :], OP.add,
                         inc=True)
                bank_free(0, sv); bank_free(1, sv)
                bslot(18.0)

        def emit_A567(ch2):
            tau0 = ch2 * NTAU
            for kd in range(NKD):
                ysl = ya[:, kd * CH:(kd + 1) * CH]
                v_stt(ysl, ysl, sc_c[:, kd:kd + 1],
                      xnT_bf[:, :].rearrange("p (kd t2) -> p kd t2", t2=G)[
                          :, kd, ch2 * CH:(ch2 + 1) * CH],
                      OP.mult, OP.add, nd=(kd < NKD - 1), inc=(kd == NKD - 1))
            a1_ready = (sv, sv.c)
            bslot(8.0)
            need_bank(0)
            for kd in range(NKD):
                pr.wait_mark("scalar", a1_ready)
                need("scalar", freed, "sqt")
                s_act(sqt[:, :], ya[:, kd * CH:(kd + 1) * CH], AF.Square)
                pr.wait("tensor", ss, ss.c)
                mm(pb[0][0:1, 0:CH], ones_c[:, 0:1], sqt[:, :], kd == 0, kd == NKD - 1,
                   inc=True)
                mark(freed, "sqt", st)
            pr.wait("scalar", st, st.c)
            s_act(r2s[:, :], pb[0][0:1, 0:CH], AF.Sqrt, scale=constv[0:1, 2:3],
                  bias=constv[0:1, 1:2])
            bank_free(0, ss)
            pr.wait("vector", ss, ss.c)
            v_recip(r2s[:, :], r2s[:, :], inc=True)
            need_bank(1)
            pr.wait("tensor", sv, sv.c)
            mm(pb[1][:, :], ones_r[0:1, 0:128], r2s[0:1, :], True, True, inc=True)
            pr.wait("vector", st, st.c)
            need("vector", freed, "xn3T_bf")
            need("vector", freed, "xn3T_bf2")
            for kd in range(NKD):
                v_stt(xn3T_bf[:, kd * G + ch2 * CH: kd * G + (ch2 + 1) * CH],
                      ya[:, kd * CH:(kd + 1) * CH], n2w_c[:, kd:kd + 1],
                      pb[1][:, :], OP.mult, OP.mult, nd=(kd < NKD - 1),
                      inc=(kd == NKD - 1))
            bank_free(1, sv)
            bslot(8.0)
            acc3 = acc[:, :].rearrange("p (t d) -> p t d", d=D)
            for kd in range(NKD):
                bnk = kd % 2
                need_bank(bnk)
                pr.wait_mark("tensor", a1_ready)
                for t_ in range(NTAU):
                    tr(pb[bnk][:, t_ * 128:(t_ + 1) * 128],
                       ya[:, kd * CH + t_ * 128: kd * CH + (t_ + 1) * 128],
                       inc=(t_ == NTAU - 1))
                pr.wait("vector", st, st.c)
                av = acc3[:, tau0:tau0 + NTAU, kd * 128:(kd + 1) * 128]
                need("vector", freed, "acc_v")
                v_tt(av, av, pb[bnk][:, 0:512], OP.add, inc=True)
                bank_free(bnk, sv)
                if kd % 2 == 1:
                    bslot(6.0)
            mark(freed, "ya", sv)

        emit_A12(0)
        r0 = emit_dtbv(0, with_b=False)
        emit_A12(1)
        emit_scan(0, *r0)
        emit_A567(0)
        r1 = emit_dtbv(1, with_b=True)
        emit_scan(1, *r1)
        emit_A567(1)

        bflush()
        markB = fence("tensor", "vector", "scalar")
        mark(freed, "xnT_bf", st)
        mark(freed, "w_sb", st)
        mark(freed, "wT_sb", st)
        for k in range(4):
            mark(freed, f"gu{k}", st)
        for k in range(2):
            mark(freed, f"wd{k}", st)
            mark(freed, f"v{k}", st)

        for q8 in range(8):
            w1e, w2e = mlpq(q8 % 2)
            need("sync", freed, f"mlpq{q8 % 2}")
            if q8 < 2:
                wait_fence("sync", markB)
                for m_ in prep_stores:
                    pr.wait_mark("sync", m_)
            pr.on("sync", lambda e, o=w1e, q8=q8: e.dma_start(
                out=o.rearrange("p (dc f) -> p dc f", f=512),
                in_=W1T_b[:, q8 * 512:(q8 + 1) * 512].rearrange(
                    "(dc p) f -> p dc f", p=128)), inc=dq[q8 % 2], n=16)
            pr.on("sync", lambda e, o=w2e, q8=q8: e.dma_start(
                out=o.rearrange("p (fc d) -> p fc d", d=D),
                in_=W2T_b[q8 * 512:(q8 + 1) * 512, :].rearrange(
                    "(fc p) d -> p fc d", p=128)), inc=dq[q8 % 2], n=16)
            qw_ready = (dq[q8 % 2], dq[q8 % 2].c)
            pr.wait_mark("tensor", qw_ready)
            need("scalar", freed, "hTm")
            if q8 == 0:
                wait_fence("scalar", markB)
                wait_fence("tensor", markB)
            for fcq in range(4):
                for th in range(2):
                    bnk = 2 + (fcq * 2 + th) % 2
                    need_bank(bnk)
                    for dc in range(NKD):
                        mm(pb[bnk][:, :],
                           w1e[:, dc * 512 + fcq * 128: dc * 512 + (fcq + 1) * 128],
                           xn3T_bf[:, dc * G + th * 512: dc * G + (th + 1) * 512],
                           dc == 0, dc == NKD - 1, inc=(dc == NKD - 1))
                    pr.wait("scalar", st, st.c)
                    s_act(hTm[:, fcq * G + th * 512: fcq * G + (th + 1) * 512],
                          pb[bnk][:, :], AF.Gelu_apprx_tanh,
                          bias=b1_c[:, q8 * 4 + fcq: q8 * 4 + fcq + 1])
                    bank_free(bnk, ss)
            hT_ready = (ss, ss.c)
            pr.wait_mark("tensor", hT_ready)
            for t_ in range(8):
                th, tq = t_ // 4, t_ % 4
                for dh in range(2):
                    bnk = 4 + (t_ * 2 + dh) % 2
                    need_bank(bnk)
                    first = True
                    if q8 == 0:
                        mm(pb[bnk][:, :], ones_b[0:1, 0:128],
                           b2_r[0:1, dh * 512:(dh + 1) * 512], True, False)
                        first = False
                    for fcq in range(4):
                        mm(pb[bnk][:, :],
                           hTm[:, fcq * G + th * 512 + tq * 128:
                               fcq * G + th * 512 + (tq + 1) * 128],
                           w2e[:, fcq * 1024 + dh * 512: fcq * 1024 + (dh + 1) * 512],
                           first and fcq == 0, fcq == 3, inc=(fcq == 3))
                        first = False
                    pr.wait("vector", st, st.c)
                    av = acc[:, t_ * D + dh * 512: t_ * D + (dh + 1) * 512]
                    need("vector", freed, "acc_v")
                    v_tt(av, av, pb[bnk][:, :], OP.add, inc=True)
                    bank_free(bnk, sv)
            mark(freed, "hTm", st)
            mark(freed, f"mlpq{q8 % 2}", st)
        markMLP = fence("tensor", "vector", "scalar")
        mark(freed, "xn3T_bf", st)
        mark(freed, "mlpq0", st)
        mark(freed, "mlpq1", st)

        pr.wait("sync", sv, sv.c)
        pr.on("sync", lambda e, g=g: e.dma_start(
            out=out[g * G:(g + 1) * G, :].rearrange("(t p) d -> p t d", p=128),
            in_=acc[:, :].rearrange("p (t d) -> p t d", d=D)), inc=dms, n=16)
        mark(freed, "acc", dms)
        mark(freed, "acc_v", dms)

    pr.wait("sync", dms, dms.c)
    pr.wait("gpsimd", dms, dms.c)

    pr.build(stack)
    nc._keepalive_stack = stack
    return nc


_NC_CACHE = None


def kernel(**inputs):
    global _NC_CACHE
    if _NC_CACHE is None:
        _NC_CACHE = build_nc()
    nc = _NC_CACHE
    x = np.ascontiguousarray(inputs["L0_attn_x"], dtype=np.float32)
    wmap = {}
    for k, v in inputs.items():
        if k == "L0_attn_x":
            continue
        name = k.replace("L0_attn_", "").replace("L0_ffn_", "")
        wmap[name] = np.ascontiguousarray(v, dtype=np.float32)
    in_maps = [dict(x=x[c], **wmap) for c in range(8)]
    res = run_bass_kernel_spmd(nc, in_maps, core_ids=list(range(8)))
    return np.stack([r["out"] for r in res.results], axis=0)
